# version 07
# baseline (speedup 1.0000x reference)
from contextlib import ExitStack

import numpy as np
import ml_dtypes

import concourse.bass as bass
import concourse.mybir as mybir
from concourse.bass_utils import run_bass_kernel_spmd
from concourse.aot_env import allow_ici_all_to_all

F32 = mybir.dt.float32
BF16 = mybir.dt.bfloat16
I32 = mybir.dt.int32
AF = mybir.ActivationFunctionType
ALU = mybir.AluOpType
AX = mybir.AxisListType.X

PI = float(np.pi)
PI_IN = float(np.nextafter(np.float32(PI), np.float32(0.0)))
INV2PI = float(np.float32(1.0 / (2 * np.pi)))
C1 = 6.28125
_r1 = 2 * np.pi - C1
C2 = float(np.float32(np.ldexp(np.round(np.ldexp(_r1, 19)), -19)))
C3 = float(np.float32(_r1 - C2))
C23 = float(np.float32(C2 + C3))
EPS = 1e-5

RIEMANN = (14.134725142, 21.022039639, 25.01085758, 30.424876126, 32.935061588,
           37.586178159, 40.918719012, 43.327073281, 48.005150881, 49.773832478,
           52.970321478, 56.446247697, 59.347044003, 60.831778525, 65.112544048,
           67.079810529, 69.546401711, 72.067157674, 75.704690699, 77.144840069,
           79.33737502, 82.910380854, 84.735492981, 87.425274613, 88.809111208,
           92.491899271, 94.651344041, 95.870634228, 98.831194218, 101.317851006,
           103.72553804, 105.446623052)


def make_cfg(S=8192, QC=1024):
    return dict(S=S, D=1024, QC=QC, NT=S // 128, NQC=S // QC, NB=QC // 512,
                NH=4, DH=64, DC=8, TQ=S // 4, NTQ=S // 4 // 128)


class Prog:

    ENGS = ("sync", "gpsimd", "vector", "scalar", "tensor")

    def __init__(self, block):
        self.block = block
        self.buf = {k: [] for k in self.ENGS}

    def on(self, eng):
        def deco(fn):
            self.buf[eng].append(fn)
            return None
        return deco

    def flush(self):
        for eng in self.ENGS:
            fns = self.buf[eng]
            if not fns:
                continue

            def section(e, fns=list(fns)):
                for f in fns:
                    f(e)
            getattr(self.block, eng)(section)
            self.buf[eng] = []


class Sem:
    def __init__(self, nc, es, name):
        self.h = es.enter_context(nc.semaphore(name))
        self.n = 0

    def inc(self, k=1):
        self.n += k
        return self.n



def emit_ln(P, s, sem_name, x_ap, xsq, lsm, epsT, sq_waits, red_waits, chain_waits=()):
    import concourse.mybir as _mb
    D = 1024

    @P.on("scalar")
    def _(e, waits=tuple(sq_waits)):
        for (ss, v) in waits:
            e.wait_ge(s[ss].h, v)
        e.activation(xsq, x_ap, AF.Square).then_inc(s[sem_name].h, 1)
    s[sem_name].inc()

    @P.on("vector")
    def _(e, w=s[sem_name].n, waits=tuple(red_waits)):
        for (ss, v) in waits:
            e.wait_ge(s[ss].h, v)
        e.wait_ge(s[sem_name].h, w)
        e.reduce_sum(lsm[:, 0:1], x_ap, axis=AX)
        e.reduce_sum(lsm[:, 1:2], xsq, axis=AX).then_inc(s[sem_name].h, 1)
    s[sem_name].inc()

    @P.on("scalar")
    def _(e, w=s[sem_name].n, waits=tuple(chain_waits)):
        for (ss, v) in waits:
            e.wait_ge(s[ss].h, v)
        e.wait_ge(s[sem_name].h, w)
        e.activation(lsm[:, 3:4], lsm[:, 0:1], AF.Square, scale=1.0 / D)
        e.activation(lsm[:, 4:5], lsm[:, 3:4], AF.Identity, scale=-1.0)
        e.activation(lsm[:, 2:3], lsm[:, 0:1], AF.Identity, scale=1.0 / D)
        e.activation(lsm[:, 9:10], lsm[:, 2:3], AF.Identity, scale=-1.0)
        e.activation(lsm[:, 5:6], lsm[:, 1:2], AF.Identity, scale=1.0 / D,
                     bias=lsm[:, 4:5])
        e.activation(lsm[:, 6:7], lsm[:, 5:6], AF.Identity, bias=epsT[:, 0:1])
        e.activation(lsm[:, 7:8], lsm[:, 6:7], AF.Ln)
        e.activation(lsm[:, 8:9], lsm[:, 7:8], AF.Exp, scale=-0.5).then_inc(s[sem_name].h, 1)
    s[sem_name].inc()


def build_program(cfg, dbg=False):
    S, D, QC = cfg["S"], cfg["D"], cfg["QC"]
    NT, NQC, NB = cfg["NT"], cfg["NQC"], cfg["NB"]
    NH, DH, DC = cfg["NH"], cfg["DH"], cfg["DC"]
    TQ, NTQ = cfg["TQ"], cfg["NTQ"]
    KPC = QC // 128

    nc = bass.Bass("TRN2", target_bir_lowering=False)

    d_x = nc.dram_tensor("x", [S, D], F32, kind="ExternalInput").ap()
    d_xres = nc.dram_tensor("xres", [TQ, D], F32, kind="ExternalInput").ap()
    d_wq = nc.dram_tensor("wq", [D, NH * DH], F32, kind="ExternalInput").ap()
    d_wk = nc.dram_tensor("wk", [D, NH * DH], F32, kind="ExternalInput").ap()
    d_wv = nc.dram_tensor("wv", [D, NH * DH], F32, kind="ExternalInput").ap()
    d_wo = nc.dram_tensor("wo", [D, D], F32, kind="ExternalInput").ap()
    d_wf = nc.dram_tensor("wf", [D, D], F32, kind="ExternalInput").ap()
    d_dsh = nc.dram_tensor("dsh", [64, D], F32, kind="ExternalInput").ap()
    d_lsc = nc.dram_tensor("lsc", [1, NH], F32, kind="ExternalInput").ap()
    d_th0 = nc.dram_tensor("th0", [128, NT * 32], F32, kind="ExternalInput").ap()
    d_idb = nc.dram_tensor("idb", [128, 128], BF16, kind="ExternalInput").ap()
    d_idf = nc.dram_tensor("idf", [128, 128], F32, kind="ExternalInput").ap()
    d_mkz = nc.dram_tensor("mkz", [128, 512], BF16, kind="ExternalInput").ap()
    d_one = nc.dram_tensor("one", [1, 128], F32, kind="ExternalInput").ap()
    d_gb = nc.dram_tensor("gb", [1, 1], I32, kind="ExternalInput").ap()
    d_out = nc.dram_tensor("out", [TQ, D], F32, kind="ExternalOutput").ap()

    t_a2i = nc.dram_tensor("a2i", [8 * NH * 64, TQ], BF16)
    t_a2o = nc.dram_tensor("a2o", [8 * NH * 64, TQ], BF16)
    if dbg:
        d_okfm = nc.dram_tensor("o_kfm", [64, NH * S], BF16, kind="ExternalOutput").ap()
        d_ovx = nc.dram_tensor("o_vx", [128, NT * NH * 65], BF16, kind="ExternalOutput").ap()
        d_onegm = nc.dram_tensor("o_negm", [128, 2 * NH], F32, kind="ExternalOutput").ap()

    es_all = ExitStack()
    E = es_all.enter_context

    s = {}
    for nm in ["x", "th", "ln", "sqx", "tpx", "xhT", "qkv", "trig", "sin",
               "rope", "sq", "nrm", "ktr", "kev", "bar", "wld", "wcast",
               "qtr", "qev", "sc", "exp", "msk", "pv", "dnm", "rcp", "bc",
               "rb", "ao", "aodma", "cc", "aold", "wfld", "wft", "weff",
               "xr", "ymm", "x1", "xh2", "tp2", "xh2T", "y2", "outp",
               "outd", "wo_ld", "wo_c", "dshld", "msc",
               "xB", "thB", "wkld", "wkldB", "wo_ldB", "wfldB", "xrB", "outdB",
               "vmm", "vev", "nfz"]:
        s[nm] = Sem(nc, es_all, "s_" + nm)

    scbc = E(nc.sbuf_tensor("scbc", [128, NH], F32))
    negM = E(nc.sbuf_tensor("negM", [128, 2 * NH], F32))
    nrm = [E(nc.sbuf_tensor(f"nrm{j}", [128, 8], F32)) for j in range(2)]
    epsT = E(nc.sbuf_tensor("epsT", [128, 1], F32))
    idb = E(nc.sbuf_tensor("sb_idb", [128, 128], BF16))
    idf = E(nc.sbuf_tensor("sb_idf", [128, 128], F32))
    mkz = E(nc.sbuf_tensor("sb_mkz", [128, 512], BF16))
    onef = E(nc.sbuf_tensor("onef", [1, 128], F32))
    lscT = E(nc.sbuf_tensor("lscT", [1, NH], F32))
    gbt = E(nc.sbuf_tensor("gbt", [1, 1], I32))

    esM = ExitStack()
    EM = esM.enter_context
    Kfm = EM(nc.sbuf_tensor("Kfm", [64, NH * S], BF16))
    Vx = EM(nc.sbuf_tensor("Vx", [128, NT * NH * 65], BF16))
    qall = [EM(nc.sbuf_tensor(f"qall{j}", [64, NH * QC], BF16)) for j in range(2)]
    wqkv = EM(nc.sbuf_tensor("wqkv", [128, DC * 768], BF16))
    xstg = [EM(nc.sbuf_tensor(f"xstg{j}", [128, D], F32)) for j in range(2)]
    xsq = EM(nc.sbuf_tensor("xsq", [128, D], F32))
    xhat = [EM(nc.sbuf_tensor(f"xhat{j}", [128, D], BF16)) for j in range(2)]
    xhT = [EM(nc.sbuf_tensor(f"xhT{j}", [128, D], BF16)) for j in range(2)]
    Qtm = [EM(nc.sbuf_tensor(f"Qtm{j}", [128, 256], BF16)) for j in range(2)]
    krot = [EM(nc.sbuf_tensor(f"krot{j}", [128, 256], BF16)) for j in range(2)]
    TB = 8
    NBATCH = (NT + TB - 1) // TB
    th0b = [EM(nc.sbuf_tensor(f"th0b{j}", [128, TB * 32], F32)) for j in range(2)]
    cosr = [EM(nc.sbuf_tensor(f"cosr{j}", [128, TB * 128], BF16)) for j in range(2)]
    sinr = [EM(nc.sbuf_tensor(f"sinr{j}", [128, TB * 128], BF16)) for j in range(2)]
    thv = EM(nc.sbuf_tensor("thv", [128, 256], F32))
    tki = EM(nc.sbuf_tensor("tki", [128, 256], I32))
    tkf = EM(nc.sbuf_tensor("tkf", [128, 256], F32))
    tg1 = EM(nc.sbuf_tensor("tg1", [128, 256], F32))
    tg2 = EM(nc.sbuf_tensor("tg2", [128, 256], F32))
    tg3 = EM(nc.sbuf_tensor("tg3", [128, 256], F32))
    tu1 = EM(nc.sbuf_tensor("tu1", [128, 256], F32))
    tu2 = EM(nc.sbuf_tensor("tu2", [128, 256], F32))
    trs = [EM(nc.sbuf_tensor(f"trs{j}", [128, 256], F32)) for j in range(2)]
    trc = [EM(nc.sbuf_tensor(f"trc{j}", [128, 256], F32)) for j in range(2)]
    tmpa = EM(nc.sbuf_tensor("tmpa", [128, 128], F32))
    tmpb = EM(nc.sbuf_tensor("tmpb", [128, 128], F32))
    sqs = [EM(nc.sbuf_tensor(f"sqs{j}", [128, 512], F32)) for j in range(2)]
    lsm = EM(nc.sbuf_tensor("lsm", [128, 16], F32))
    nrw = [EM(nc.sbuf_tensor(f"nrw{j}", [128, 8], F32)) for j in range(2)]
    nfin = EM(nc.sbuf_tensor("nfin", [128, 8], F32))
    nfs = EM(nc.sbuf_tensor("nfs", [128, 512], F32))
    pt = [EM(nc.sbuf_tensor(f"pt{j}", [128, QC], BF16)) for j in range(3)]
    aos = [EM(nc.sbuf_tensor(f"aos{j}", [64, QC], BF16)) for j in range(2)]
    dnm = EM(nc.sbuf_tensor("dnm", [1, QC], F32))
    rcpr = EM(nc.sbuf_tensor("rcpr", [1, 2 * QC], F32))
    rbt = EM(nc.sbuf_tensor("rbt", [64, 512], F32))
    ptm = EM(nc.sbuf_tensor("ptm", [128, 512], BF16))
    ps_sc = [EM(nc.psum_tensor(f"ps_sc{j}", [128, QC], F32)) for j in range(2)]
    ps_pv = EM(nc.psum_tensor("ps_pv", [65, QC], F32))
    ps_f32 = EM(nc.psum_tensor("ps_f32", [128, 512], F32))
    ps_bf = EM(nc.psum_tensor("ps_bf", [128, D], BF16))

    block_cm = nc.Block()
    block = block_cm.__enter__()
    P = Prog(block)

    es0 = ExitStack()
    E0 = es0.enter_context
    wstg = [E0(nc.sbuf_tensor("wstg0", [128, 768], F32))]

    @P.on("sync")
    def _(e):
        for dst, src in ((idb, d_idb), (idf, d_idf), (mkz, d_mkz),
                         (onef, d_one), (lscT, d_lsc), (gbt, d_gb)):
            e.dma_start(out=dst[:, :], in_=src[:, :]).then_inc(s["wld"].h, 16)
    s["wld"].inc(96)

    @P.on("scalar")
    def _(e, w=s["wld"].n):
        e.wait_ge(s["wld"].h, w)
        e.activation(lscT[:, :], lscT[:, :], AF.Exp).then_inc(s["msc"].h, 1)
    s["msc"].inc()

    @P.on("tensor")
    def _(e, w=s["msc"].n):
        e.wait_ge(s["msc"].h, w)
        e.matmul(ps_f32[:, 0:NH], onef[0:1, :], lscT[0:1, :], start=True,
                 stop=True).then_inc(s["msc"].h, 1)
    s["msc"].inc()

    @P.on("vector")
    def _(e, w=s["msc"].n):
        e.wait_ge(s["msc"].h, w)
        e.tensor_copy(scbc[:, :], ps_f32[:, 0:NH])
        e.memset(bass.AP(Vx, 64, [[NT * NH * 65, 128], [65, NT * NH], [1, 1]]), 1.0)
        e.memset(nrm[0][:, :], 0.0)
        e.memset(nrm[1][:, :], 0.0)
        e.memset(epsT[:, :], EPS).then_inc(s["msc"].h, 1)
    s["msc"].inc()
    MSC_ALL = s["msc"].n

    for c in range(DC):
        stg = wstg[0]

        sm = "wkld" if c % 2 == 0 else "wkldB"

        @P.on("sync")
        def _(e, c=c, stg=stg, sm=sm, wfree=s["wcast"].n):
            if wfree > 0:
                e.wait_ge(s["wcast"].h, wfree)
            e.dma_start(out=stg[:, 0:256], in_=d_wq[c * 128:(c + 1) * 128, :]).then_inc(s[sm].h, 16)
            e.dma_start(out=stg[:, 256:512], in_=d_wk[c * 128:(c + 1) * 128, :]).then_inc(s[sm].h, 16)
            e.dma_start(out=stg[:, 512:768], in_=d_wv[c * 128:(c + 1) * 128, :]).then_inc(s[sm].h, 16)
        s[sm].inc(48)

        @P.on("vector")
        def _(e, c=c, stg=stg, sm=sm, w=s[sm].n):
            e.wait_ge(s[sm].h, w)
            e.tensor_copy(wqkv[:, c * 768:(c + 1) * 768], stg[:, :]).then_inc(s["wcast"].h, 1)
        s["wcast"].inc()

    def trig_nsub(b):
        ntl = min(TB, NT - b * TB)
        return (ntl * 128 + 255) // 256

    def emit_trig_sub(b, sub):
        ntl = min(TB, NT - b * TB)
        smt = "th" if b % 2 == 0 else "thB"
        if sub == 0:
            @P.on("sync")
            def _(e, b=b, ntl=ntl, smt=smt, wtr=s["trig"].n):
                if b >= 2:
                    e.wait_ge(s["trig"].h, wtr)
                e.dma_start(out=th0b[b % 2][:, 0:ntl * 32],
                            in_=d_th0[:, b * TB * 32:(b * TB + ntl) * 32]).then_inc(s[smt].h, 16)
            s[smt].inc(16)
        ncols = min(256, ntl * 128 - sub * 256)

        @P.on("vector")
        def _(e, b=b, sub=sub, ncols=ncols, smt=smt, wth=s[smt].n, wm=s["msc"].n,
              wtr=s["trig"].n - 1):
            e.wait_ge(s[smt].h, wth)
            e.wait_ge(s["msc"].h, wm)
            if wtr > 0:
                e.wait_ge(s["trig"].h, wtr)
            if b >= 2:
                e.wait_ge(s["rope"].h, (b - 1) * TB)
            ntile4 = ncols // 128
            for h in range(NH):
                ov = bass.AP(thv, h * 32, [[256, 128], [128, ntile4], [1, 32]])
                iv = bass.AP(th0b[b % 2], sub * 64, [[TB * 32, 128], [32, ntile4], [1, 32]])
                e.tensor_scalar(ov, iv, scbc[:, h:h + 1], None, ALU.mult)
            c = slice(0, ncols)
            e.tensor_scalar(tu1[:, c], thv[:, c], INV2PI, None, ALU.mult)
            e.tensor_copy(tki[:, c], tu1[:, c])
            e.tensor_copy(tkf[:, c], tki[:, c])
            e.scalar_tensor_tensor(tu1[:, c], tkf[:, c], -C1, thv[:, c], ALU.mult, ALU.add)
            e.scalar_tensor_tensor(tu2[:, c], tkf[:, c], -C2, tu1[:, c], ALU.mult, ALU.add)
            e.scalar_tensor_tensor(tg1[:, c], tkf[:, c], -C3, tu2[:, c], ALU.mult, ALU.add)
            e.tensor_scalar(tg2[:, c], tg1[:, c], PI, None, ALU.is_gt)
            e.tensor_scalar(tg3[:, c], tg1[:, c], -PI, None, ALU.is_lt)
            e.tensor_tensor(tu1[:, c], tg3[:, c], tg2[:, c], ALU.subtract)
            e.scalar_tensor_tensor(tu2[:, c], tu1[:, c], C1, tg1[:, c], ALU.mult, ALU.add)
            e.scalar_tensor_tensor(tg3[:, c], tu1[:, c], C23, tu2[:, c], ALU.mult, ALU.add)
            e.tensor_scalar(trs[sub % 2][:, c], tg3[:, c], -PI_IN, PI_IN, ALU.max, ALU.min)
            e.tensor_scalar(tu1[:, c], trs[sub % 2][:, c], PI / 2, None, ALU.add)
            e.tensor_scalar(tg2[:, c], tu1[:, c], PI, None, ALU.is_gt)
            e.scalar_tensor_tensor(tu2[:, c], tg2[:, c], -C1, tu1[:, c], ALU.mult, ALU.add)
            e.scalar_tensor_tensor(tg3[:, c], tg2[:, c], -C23, tu2[:, c], ALU.mult, ALU.add)
            e.tensor_scalar(trc[sub % 2][:, c], tg3[:, c], -PI_IN, PI_IN,
                            ALU.max, ALU.min).then_inc(s["sin"].h, 1)
        s["sin"].inc()

        @P.on("scalar")
        def _(e, b=b, sub=sub, ncols=ncols, w=s["sin"].n):
            e.wait_ge(s["sin"].h, w)
            e.activation(sinr[b % 2][:, sub * 256:sub * 256 + ncols],
                         trs[sub % 2][:, 0:ncols], AF.Sin)
            e.activation(cosr[b % 2][:, sub * 256:sub * 256 + ncols],
                         trc[sub % 2][:, 0:ncols], AF.Sin).then_inc(s["trig"].h, 1)
        s["trig"].inc()

    pref_qev, pref_kev, pref_vev, pref_nfz = {}, {}, {}, {}
    pref_sc2, pref_exp = {}, {}
    chst = dict(pend_tail=[], pend_fin=None, sc_free=[0, 0], pt_free=[0, 0, 0])
    PVLAG = 2

    def tile_gen(i):
        j = i % 2
        if i == 0:
            for _sub in range(trig_nsub(0)):
                emit_trig_sub(0, _sub)
        bnext = i // TB + 1
        if bnext < NBATCH and i % TB >= TB // 2:
            _sub = i % TB - TB // 2
            if _sub < trig_nsub(bnext):
                emit_trig_sub(bnext, _sub)

        smx = "x" if j == 0 else "xB"

        @P.on("gpsimd")
        def _(e, i=i, j=j, smx=smx):
            if i >= 2:
                e.wait_ge(s["ln"].h, i - 1)
            e.dma_start(out=xstg[j][:, :], in_=d_x[i * 128:(i + 1) * 128, :]).then_inc(s[smx].h, 16)
        s[smx].inc(16)

        sq_waits = [(smx, s[smx].n)] + ([("sqx", s["sqx"].n)] if i >= 1 else [])
        red_waits = [("tpx", s["tpx"].n - 1)] if i >= 2 else []
        emit_ln(P, s, "sqx", xstg[j][:, :], xsq[:, :], lsm, epsT, sq_waits, red_waits,
                chain_waits=([("ln", i)] if i >= 1 else []))

        @P.on("vector")
        def _(e, j=j, w=s["sqx"].n):
            e.wait_ge(s["sqx"].h, w)
            e.tensor_scalar(xhat[j][:, :], xstg[j][:, :], lsm[:, 9:10], lsm[:, 8:9],
                            ALU.add, ALU.mult).then_inc(s["ln"].h, 1)
        s["ln"].inc()

        yield

        @P.on("tensor")
        def _(e, i=i, j=j, wl=s["ln"].n, wx=s["xhT"].n, wk=s["kev"].n, wq=s["qev"].n):
            e.wait_ge(s["ln"].h, wl)
            if i >= 1:
                e.wait_ge(s["xhT"].h, wx)
                if wk > 0:
                    e.wait_ge(s["kev"].h, wk)
                if wq > 0:
                    e.wait_ge(s["qev"].h, wq)
            last = None
            for c in range(DC):
                last = e.transpose(ps_bf[:, c * 128:(c + 1) * 128],
                                   xhat[j][:, c * 128:(c + 1) * 128], idb[:, :])
            last.then_inc(s["tpx"].h, 1)
        s["tpx"].inc()

        @P.on("scalar")
        def _(e, j=j, w=s["tpx"].n):
            e.wait_ge(s["tpx"].h, w)
            e.copy(xhT[j][:, :], ps_bf[:, :]).then_inc(s["xhT"].h, 1)
        s["xhT"].inc()

        @P.on("tensor")
        def _(e, i=i, j=j, wx=s["xhT"].n, wr=s["rope"].n, wv=s["vev"].n,
              wrb=s["rb"].n, wnf=s["nfz"].n, wc=s["wcast"].n, wm=MSC_ALL):
            e.wait_ge(s["xhT"].h, wx)
            if wr > 0:
                e.wait_ge(s["rope"].h, wr)
            if wv > 0:
                e.wait_ge(s["vev"].h, wv)
            if wrb > 0:
                e.wait_ge(s["rb"].h, wrb)
            if wnf > 0:
                e.wait_ge(s["nfz"].h, wnf)
            if i == 0:
                e.wait_ge(s["msc"].h, wm)
            if i >= 2:
                e.wait_ge(s["wcast"].h, wc)
            last = None
            for c in range(DC):
                if i < 2:
                    e.wait_ge(s["wcast"].h, c + 1)
                st, sp = (c == 0), (c == DC - 1)
                last = e.matmul(ps_f32[:, 0:512], xhT[j][:, c * 128:(c + 1) * 128],
                                wqkv[:, c * 768:c * 768 + 512], start=st, stop=sp)
            last.then_inc(s["qkv"].h, 1)
        s["qkv"].inc()

        yield

        @P.on("vector")
        def _(e, i=i, j=j, wq_=s["qkv"].n, ws=s["trig"].n, wk=s["ktr"].n,
              wqt=s["qtr"].n, wsq=s["sq"].n):
            e.wait_ge(s["qkv"].h, wq_)
            e.wait_ge(s["trig"].h, ws)
            if i >= 2:
                e.wait_ge(s["ktr"].h, wk - 1)
                e.wait_ge(s["qtr"].h, wqt - 1)
                e.wait_ge(s["sq"].h, wsq - 1)
            rb, ro = (i // TB) % 2, (i % TB) * 128
            cosv = bass.AP(cosr[rb], ro, [[TB * 128, 128], [32, NH], [1, 32]])
            sinv = bass.AP(sinr[rb], ro, [[TB * 128, 128], [32, NH], [1, 32]])
            last = None
            for (base, dst) in ((0, Qtm[j]), (256, krot[j])):
                x1 = bass.AP(ps_f32, base, [[512, 128], [64, NH], [2, 32]])
                x2 = bass.AP(ps_f32, base + 1, [[512, 128], [64, NH], [2, 32]])
                o1 = bass.AP(dst, 0, [[256, 128], [64, NH], [1, 32]])
                o2 = bass.AP(dst, 32, [[256, 128], [64, NH], [1, 32]])
                va = bass.AP(tmpa, 0, [[128, 128], [32, NH], [1, 32]])
                vb = bass.AP(tmpb, 0, [[128, 128], [32, NH], [1, 32]])
                e.tensor_tensor(va, x1, cosv, ALU.mult)
                e.tensor_tensor(vb, x2, sinv, ALU.mult)
                e.tensor_tensor(o1, va, vb, ALU.subtract)
                e.tensor_tensor(va, x1, sinv, ALU.mult)
                e.tensor_tensor(vb, x2, cosv, ALU.mult)
                last = e.tensor_tensor(o2, va, vb, ALU.add)
            last.then_inc(s["rope"].h, 1)
        s["rope"].inc()

        @P.on("tensor")
        def _(e, j=j, wr=s["rope"].n, wrb=s["rb"].n, wnf=s["nfz"].n):
            e.wait_ge(s["rope"].h, wr)
            if wrb > 0:
                e.wait_ge(s["rb"].h, wrb)
            if wnf > 0:
                e.wait_ge(s["nfz"].h, wnf)
            last = None
            for c in range(DC):
                st, sp = (c == 0), (c == DC - 1)
                last = e.matmul(ps_f32[:, 0:256], xhT[j][:, c * 128:(c + 1) * 128],
                                wqkv[:, c * 768 + 512:c * 768 + 768], start=st, stop=sp)
            last.then_inc(s["vmm"].h, 1)
        s["vmm"].inc()

        @P.on("vector")
        def _(e, i=i, w=s["vmm"].n):
            e.wait_ge(s["vmm"].h, w)
            vin = bass.AP(ps_f32, 0, [[512, 128], [64, NH], [1, 64]])
            vout = bass.AP(Vx, i * NH * 65, [[NT * NH * 65, 128], [65, NH], [1, 64]])
            e.tensor_copy(vout, vin).then_inc(s["vev"].h, 1)
        s["vev"].inc()

        @P.on("scalar")
        def _(e, i=i, j=j, w=s["rope"].n, wn=s["nrm"].n):
            e.wait_ge(s["rope"].h, w)
            if i >= 2:
                e.wait_ge(s["nrm"].h, wn - 1)
            e.activation(sqs[j][:, 0:256], Qtm[j][:, :], AF.Square)
            e.activation(sqs[j][:, 256:512], krot[j][:, :], AF.Square).then_inc(s["sq"].h, 1)
        s["sq"].inc()

        yield

        @P.on("vector")
        def _(e, i=i, j=j, w=s["sq"].n):
            e.wait_ge(s["sq"].h, w)
            v3 = bass.AP(sqs[j], 0, [[512, 128], [64, 8], [1, 64]])
            e.tensor_reduce(nrw[i % 2][:, :], v3, AX, ALU.add)
            if i >= 1:
                e.tensor_tensor(nrm[(i - 1) % 2][:, :], nrm[i % 2][:, :],
                                nrw[(i - 1) % 2][:, :], ALU.max)
            e.nop().then_inc(s["nrm"].h, 1)
        s["nrm"].inc()

        @P.on("tensor")
        def _(e, j=j, wr=s["rope"].n, wx=s["xhT"].n):
            e.wait_ge(s["rope"].h, wr)
            e.wait_ge(s["xhT"].h, wx)
            last = None
            for h in range(NH):
                last = e.transpose(ps_bf[0:64, h * 128:(h + 1) * 128],
                                   krot[j][:, h * 64:(h + 1) * 64], idb[:, :])
            last.then_inc(s["ktr"].h, 1)
        s["ktr"].inc()

        @P.on("vector")
        def _(e, i=i, w=s["ktr"].n):
            e.wait_ge(s["ktr"].h, w)
            kin = bass.AP(ps_bf, 0, [[1024, 64], [128, NH], [1, 128]])
            kout = bass.AP(Kfm, i * 128, [[NH * S, 64], [S, NH], [1, 128]])
            e.tensor_copy(kout, kin).then_inc(s["kev"].h, 1)
        s["kev"].inc()

        @P.on("tensor")
        def _(e, j=j, wr=s["rope"].n, wx=s["xhT"].n):
            e.wait_ge(s["rope"].h, wr)
            e.wait_ge(s["xhT"].h, wx)
            last = None
            for h in range(NH):
                last = e.transpose(ps_bf[0:64, 512 + h * 128:512 + (h + 1) * 128],
                                   Qtm[j][:, h * 64:(h + 1) * 64], idb[:, :])
            last.then_inc(s["qtr"].h, 1)
        s["qtr"].inc()

        @P.on("vector")
        def _(e, i=i, w=s["qtr"].n, wsc=pref_sc2.get(i // KPC - 2, 0)):
            e.wait_ge(s["qtr"].h, w)
            if wsc > 0:
                e.wait_ge(s["sc"].h, wsc)
            qin = bass.AP(ps_bf, 512, [[1024, 64], [128, NH], [1, 128]])
            qout = bass.AP(qall[(i // KPC) % 2], (i % KPC) * 128,
                           [[NH * QC, 64], [QC, NH], [1, 128]])
            e.tensor_copy(qout, qin).then_inc(s["qev"].h, 1)
        s["qev"].inc()

    def emit_finalize(q):
        t = (q + 1) * KPC - 1

        @P.on("vector")
        def _(e, t=t, w=s["nrm"].n):
            e.wait_ge(s["nrm"].h, w)
            e.tensor_tensor(nfin[:, :], nrm[(t - 1) % 2][:, :], nrw[t % 2][:, :],
                            ALU.max).then_inc(s["bar"].h, 1)
        s["bar"].inc()

        @P.on("tensor")
        def _(e, w=s["bar"].n, wv=s["vev"].n, wrb=s["rb"].n):
            e.wait_ge(s["bar"].h, w)
            e.wait_ge(s["vev"].h, wv)
            if wrb > 0:
                e.wait_ge(s["rb"].h, wrb)
            e.transpose(ps_f32[0:8, 0:128], nfin[:, 0:8], idf[:, :]).then_inc(s["bar"].h, 1)
        s["bar"].inc()

        @P.on("vector")
        def _(e, w=s["bar"].n):
            e.wait_ge(s["bar"].h, w)
            e.tensor_copy(nfs[0:8, 0:128], ps_f32[0:8, 0:128])
            e.tensor_copy(nfs[0:16, 256:384], idf[0:16, 0:128])
            e.reduce_max(nfs[0:8, 128:129], nfs[0:8, 0:128], axis=AX).then_inc(s["bar"].h, 1)
        s["bar"].inc()

        @P.on("scalar")
        def _(e, w=s["bar"].n):
            e.wait_ge(s["bar"].h, w)
            e.activation(nfs[0:8, 129:130], nfs[0:8, 128:129], AF.Sqrt).then_inc(s["bar"].h, 1)
        s["bar"].inc()

        @P.on("tensor")
        def _(e, w=s["bar"].n):
            e.wait_ge(s["bar"].h, w)
            e.transpose(ps_f32[0:1, 0:8], nfs[0:8, 129:130], idf[0:8, 0:8]).then_inc(s["bar"].h, 1)
        s["bar"].inc()

        @P.on("vector")
        def _(e, w=s["bar"].n):
            e.wait_ge(s["bar"].h, w)
            e.tensor_copy(nfs[0:1, 132:140], ps_f32[0:1, 0:8])
            e.tensor_copy(nfs[0:16, 256:384], idf[0:16, 0:128])
            e.tensor_copy(nfs[0:16, 384:512], idf[0:16, 0:128])
            e.tensor_tensor(nfs[0:1, 140:140 + NH], nfs[0:1, 132:132 + NH],
                            nfs[0:1, 136:136 + NH], ALU.mult)
            e.tensor_copy(nfs[0:16, 256:384], idf[0:16, 0:128])
            e.tensor_copy(nfs[0:16, 384:512], idf[0:16, 0:128])
            e.tensor_scalar(nfs[0:1, 148:148 + NH], nfs[0:1, 140:140 + NH],
                            -0.125, None, ALU.mult)
            e.tensor_copy(nfs[0:16, 256:384], idf[0:16, 0:128])
            e.tensor_copy(nfs[0:16, 384:512], idf[0:16, 0:128]).then_inc(s["bar"].h, 1)
        s["bar"].inc()

        @P.on("tensor")
        def _(e, w=s["bar"].n):
            e.wait_ge(s["bar"].h, w)
            e.matmul(ps_f32[:, 0:NH], onef[0:1, :], nfs[0:1, 148:148 + NH],
                     start=True, stop=True).then_inc(s["bar"].h, 1)
        s["bar"].inc()

        @P.on("vector")
        def _(e, q=q, w=s["bar"].n, we=pref_exp.get(q - 2, 0)):
            e.wait_ge(s["bar"].h, w)
            if we > 0:
                e.wait_ge(s["exp"].h, we)
            e.tensor_copy(negM[:, (q % 2) * NH:(q % 2) * NH + NH],
                          ps_f32[:, 0:NH]).then_inc(s["nfz"].h, 1)
        s["nfz"].inc()

        pref_qev[q] = s["qev"].n
        pref_kev[q] = s["kev"].n
        pref_vev[q] = s["vev"].n
        pref_nfz[q] = s["nfz"].n

    def emit_tail_piece(args):
        kind, h, qc, cpar, o = args
        if kind == "bc":
            @P.on("tensor")
            def _(e, o=o, w=s["rcp"].n, wrb=s["rb"].n, wv=s["vmm"].n,
                  wr=s["qkv"].n, wnf=s["nfz"].n):
                e.wait_ge(s["rcp"].h, w)
                if wrb > 0:
                    e.wait_ge(s["rb"].h, wrb)
                e.wait_ge(s["vev"].h, wv)
                e.wait_ge(s["rope"].h, wr)
                if wnf > 0:
                    e.wait_ge(s["nfz"].h, wnf)
                e.matmul(ps_f32[0:64, 0:512], onef[0:1, 0:64],
                         rcpr[0:1, QC + o:QC + o + 512],
                         start=True, stop=True).then_inc(s["bc"].h, 1)
            s["bc"].inc()

            @P.on("vector")
            def _(e, cpar=cpar, o=o, w=s["bc"].n, wdma=s["aodma"].n):
                e.wait_ge(s["bc"].h, w)
                if wdma > 0:
                    e.wait_ge(s["aodma"].h, wdma)
                e.tensor_copy(rbt[:, :], ps_f32[0:64, 0:512]).then_inc(s["rb"].h, 1)
                e.tensor_tensor(aos[cpar][:, o:o + 512], ps_pv[0:64, o:o + 512],
                                rbt[:, :], ALU.mult).then_inc(s["ao"].h, 1)
            s["rb"].inc()
            s["ao"].inc()
        else:
            @P.on("gpsimd")
            def _(e, h=h, qc=qc, cpar=cpar, w=s["ao"].n):
                e.wait_ge(s["ao"].h, w)
                quarter = qc // (NQC // 4)
                coff = (qc % (NQC // 4)) * QC
                for jj in (quarter, 4 + quarter):
                    e.dma_start(
                        out=bass.AP(t_a2i, (jj * NH * 64 + h * 64) * TQ + coff,
                                    [[TQ, 64], [1, QC]]),
                        in_=aos[cpar][:, :]).then_inc(s["aodma"].h, 16)
            s["aodma"].inc(32)

    def emit_pend_fin():
        pf = chst["pend_fin"]
        if pf is None:
            return
        chst["pend_fin"] = None
        h, qc, cpar = pf

        @P.on("vector")
        def _(e, w=s["pv"].n, wr=s["rcp"].n):
            e.wait_ge(s["pv"].h, w)
            if wr > 0:
                e.wait_ge(s["rcp"].h, wr)
            e.tensor_copy(dnm[0:1, :], ps_pv[64:65, :]).then_inc(s["dnm"].h, 1)
        s["dnm"].inc()

        @P.on("scalar")
        def _(e, w=s["dnm"].n, wbc=s["bc"].n):
            e.wait_ge(s["dnm"].h, w)
            if wbc > 0:
                e.wait_ge(s["bc"].h, wbc)
            e.activation(rcpr[0:1, 0:QC], dnm[0:1, :], AF.Ln)
            e.activation(rcpr[0:1, QC:2 * QC], rcpr[0:1, 0:QC], AF.Exp,
                         scale=-1.0).then_inc(s["rcp"].h, 1)
        s["rcp"].inc()

        chst["pend_tail"] = [("bc", h, qc, cpar, half * 512) for half in range(NB)]
        chst["pend_tail"].append(("dma", h, qc, cpar, 0))

    def chunk_gen(c):
        emit_pend_fin()
        qc, h = c // NH, c % NH
        cpar = c % 2
        nkb = (qc + 1) * KPC
        dstart = qc * KPC
        order = list(range(dstart, nkb)) + list(range(0, dstart))
        first_b, last_b = {}, {}
        for ki, kb in enumerate(order):
            diag = kb >= dstart
            b0 = ((kb - dstart) * 128) // 512 if diag else 0
            for b in range(b0, NB):
                if b not in first_b:
                    first_b[b] = ki
                last_b[b] = ki
        pend_pv = []

        def emit_pv(spec):
            ki, kb, off, nblk, dep = spec
            kpar4 = ki % 3

            @P.on("tensor")
            def _(e, dep=dep, ki=ki, kb=kb, off=off, nblk=nblk, kpar4=kpar4,
                  wao=s["ao"].n if ki == 0 else 0,
                  wvv=pref_vev[qc] if ki == 0 else 0, wm=MSC_ALL):
                e.wait_ge(s[dep[0]].h, dep[1])
                if ki == 0:
                    if wao > 0:
                        e.wait_ge(s["ao"].h, wao)
                    e.wait_ge(s["vev"].h, wvv)
                    e.wait_ge(s["msc"].h, wm)
                last = None
                for b in range(nblk):
                    o = off + b * 512
                    bi = o // 512
                    last = e.matmul(ps_pv[:, o:o + 512],
                                    Vx[:, kb * NH * 65 + h * 65: kb * NH * 65 + h * 65 + 65],
                                    pt[kpar4][:, o:o + 512],
                                    start=(first_b[bi] == ki), stop=(last_b[bi] == ki))
                last.then_inc(s["pv"].h, 1)
            s["pv"].inc()
            chst["pt_free"][kpar4] = s["pv"].n

        for ki, kb in enumerate(order):
            kpar = ki % 2
            kpar4 = ki % 3
            diag = kb >= dstart
            c0 = (kb - dstart) * 128 if diag else 0
            b0 = (c0 // 512) if diag else 0
            off = 512 * b0
            nblk = NB - b0

            @P.on("tensor")
            def _(e, h=h, qc=qc, kb=kb, ki=ki, kpar=kpar, off=off, nblk=nblk,
                  wexp=chst["sc_free"][kpar],
                  wqv=pref_qev[qc] if ki == 0 else 0,
                  wkv=pref_kev[qc] if ki == 0 else 0):
                if ki == 0:
                    e.wait_ge(s["qev"].h, wqv)
                    e.wait_ge(s["kev"].h, wkv)
                if wexp > 0:
                    e.wait_ge(s["exp"].h, wexp)
                last = None
                for b in range(nblk):
                    o = off + b * 512
                    last = e.matmul(ps_sc[kpar][:, o:o + 512],
                                    Kfm[:, h * S + kb * 128: h * S + (kb + 1) * 128],
                                    qall[qc % 2][:, h * QC + o: h * QC + o + 512],
                                    start=True, stop=True)
                last.then_inc(s["sc"].h, 1)
            s["sc"].inc()

            if chst["pend_tail"] and 1 <= ki <= len(chst["pend_tail"]):
                emit_tail_piece(chst["pend_tail"][ki - 1])
                if ki == len(chst["pend_tail"]):
                    chst["pend_tail"] = []

            @P.on("scalar")
            def _(e, h=h, qc=qc, ki=ki, kpar=kpar, kpar4=kpar4, off=off,
                  w=s["sc"].n, wpv=chst["pt_free"][kpar4],
                  wnf=pref_nfz[qc] if ki == 0 else 0):
                if ki == 0:
                    e.wait_ge(s["nfz"].h, wnf)
                e.wait_ge(s["sc"].h, w)
                if wpv > 0:
                    e.wait_ge(s["pv"].h, wpv)
                e.activation(pt[kpar4][:, off:QC], ps_sc[kpar][:, off:QC], AF.Exp,
                             bias=negM[:, (qc % 2) * NH + h:(qc % 2) * NH + h + 1],
                             scale=0.125).then_inc(s["exp"].h, 1)
            s["exp"].inc()
            chst["sc_free"][kpar] = s["exp"].n

            dep = ("exp", s["exp"].n)
            if diag:
                L = c0 + 128 - off

                @P.on("vector")
                def _(e, kpar4=kpar4, off=off, L=L, w=s["exp"].n):
                    e.wait_ge(s["exp"].h, w)
                    e.tensor_tensor(ptm[:, 0:L], pt[kpar4][:, off:off + L],
                                    mkz[:, 512 - L:512], ALU.mult)
                    e.tensor_copy(pt[kpar4][:, off:off + L], ptm[:, 0:L]).then_inc(s["msk"].h, 1)
                s["msk"].inc()
                dep = ("msk", s["msk"].n)

            pend_pv.append((ki, kb, off, nblk, dep))
            if len(pend_pv) > PVLAG:
                emit_pv(pend_pv.pop(0))
            yield

        for piece in chst["pend_tail"]:
            emit_tail_piece(piece)
        chst["pend_tail"] = []
        for spec in pend_pv:
            emit_pv(spec)

        chst["pend_fin"] = (h, qc, cpar)
        pref_exp[qc] = s["exp"].n
        if h == NH - 1:
            pref_sc2[qc] = s["sc"].n

    ready = []
    sched = dict(active=None, done=0)
    GAPCAP = 3 * KPC
    GAP_IN0 = 0
    GAP_IN1 = 0

    def emit_quantum(cap):
        n = 0
        while n < cap and (sched["active"] is not None or ready):
            if sched["active"] is None:
                sched["active"] = chunk_gen(ready.pop(0))
            try:
                next(sched["active"])
                n += 1
            except StopIteration:
                sched["active"] = None
                sched["done"] += 1

    def force_chunks(n_done):
        while sched["done"] < n_done and (sched["active"] is not None or ready):
            emit_quantum(1)

    tgs = [tile_gen(i) for i in range(NT)]
    next(tgs[0])
    for i in range(NT):
        if i % KPC == 0:
            force_chunks((i // KPC - 1) * NH)
        next(tgs[i])
        if i + 1 < NT:
            next(tgs[i + 1])
        emit_quantum(GAP_IN0)
        next(tgs[i])
        emit_quantum(GAP_IN1)
        for _ in tgs[i]:
            pass
        if (i + 1) % KPC == 0:
            q = (i + 1) // KPC - 1
            force_chunks((q - 1) * NH)
            emit_finalize(q)
            for h in range(NH):
                ready.append(q * NH + h)
        emit_quantum(GAPCAP)
    emit_quantum(10 ** 9)
    emit_pend_fin()
    for piece in chst["pend_tail"]:
        emit_tail_piece(piece)
    chst["pend_tail"] = []

    if dbg:
        @P.on("gpsimd")
        def _(e, w=s["nfz"].n, w2=s["kev"].n, w3=s["vev"].n):
            e.wait_ge(s["nfz"].h, w)
            e.wait_ge(s["kev"].h, w2)
            e.wait_ge(s["vev"].h, w3)
            e.dma_start(out=d_okfm[:, :], in_=Kfm[:, :]).then_inc(s["outd"].h, 16)
            e.dma_start(out=d_ovx[:, :], in_=Vx[:, :]).then_inc(s["outd"].h, 16)
            e.dma_start(out=d_onegm[:, :], in_=negM[:, :]).then_inc(s["outd"].h, 16)
        s["outd"].inc(48)

    p3_done = dict(aodma=s["aodma"].n)
    P.flush()
    es0.close()
    esM.close()

    es4 = ExitStack()
    E4 = es4.enter_context
    wfst = [E4(nc.sbuf_tensor(f"wfst{j}", [128, 1024], F32)) for j in range(2)]
    wo_bf = E4(nc.sbuf_tensor("wo_bf", [128, DC * 1024], BF16))
    wost = [E4(nc.sbuf_tensor(f"wost{j}", [128, 1024], F32)) for j in range(2)]
    dsh2 = E4(nc.sbuf_tensor("dsh2", [128, 1024], F32))
    weff = E4(nc.sbuf_tensor("weff", [128, DC * 1024], BF16))
    ao = E4(nc.sbuf_tensor("ao", [128, DC * TQ], BF16))
    x1 = E4(nc.sbuf_tensor("x1", [128, NTQ * D], F32))
    xrs = [E4(nc.sbuf_tensor(f"xrs{j}", [128, D], F32)) for j in range(2)]
    ysq2 = E4(nc.sbuf_tensor("ysq2", [128, D], F32))
    ysq3 = E4(nc.sbuf_tensor("ysq3", [128, D], F32))
    ysq4 = E4(nc.sbuf_tensor("ysq4", [128, D], F32))
    lsm2 = E4(nc.sbuf_tensor("lsm2", [128, 16], F32))
    lsm3 = E4(nc.sbuf_tensor("lsm3", [128, 16], F32))
    lsm4 = E4(nc.sbuf_tensor("lsm4", [128, 16], F32))
    xh2 = [E4(nc.sbuf_tensor(f"xh2{j}", [128, D], BF16)) for j in range(2)]
    xh2T = [E4(nc.sbuf_tensor(f"xh2T{j}", [128, D], BF16)) for j in range(2)]
    outt = [E4(nc.sbuf_tensor(f"outt{j}", [128, D], F32)) for j in range(2)]
    ps_y = [E4(nc.psum_tensor(f"ps_y{j}", [128, 1024], F32)) for j in range(2)]
    ps_t2 = [E4(nc.psum_tensor(f"ps_t2{j}", [128, D], BF16)) for j in range(2)]
    ps_y5 = E4(nc.psum_tensor("ps_y5", [128, 1024], F32))

    @P.on("gpsimd")
    def _(e, w=p3_done["aodma"]):
        e.wait_ge(s["aodma"].h, w)
        with allow_ici_all_to_all():
            e.collective_compute(
                "AllToAll", ALU.bypass,
                replica_groups=[[0, 1, 2, 3, 4, 5, 6, 7]],
                ins=[t_a2i.ap().opt()],
                outs=[t_a2o.ap().opt()],
            ).then_inc(s["cc"].h, 1)
    s["cc"].inc()

    @P.on("sync")
    def _(e, w=p3_done["aodma"]):
        e.wait_ge(s["aodma"].h, w)
        e.dma_start(out=dsh2[0:64, :], in_=d_dsh[:, :]).then_inc(s["dshld"].h, 16)
        e.dma_start(out=dsh2[64:128, :], in_=d_dsh[:, :]).then_inc(s["dshld"].h, 16)
    s["dshld"].inc(32)

    for jj in range(DC):
        @P.on("sync")
        def _(e, jj=jj, smf="wfld" if jj % 2 == 0 else "wfldB", wfree=s["wft"].n - 1):
            if wfree > 0:
                e.wait_ge(s["wft"].h, wfree)
            e.dma_start(out=wfst[jj % 2][:, :],
                        in_=d_wf[jj * 128:(jj + 1) * 128, :]).then_inc(s[smf].h, 16)
        smf = "wfld" if jj % 2 == 0 else "wfldB"
        s[smf].inc(16)

        @P.on("tensor")
        def _(e, jj=jj, smf=smf, w=s[smf].n, wv=s["weff"].n - 1):
            e.wait_ge(s[smf].h, w)
            if jj >= 2:
                e.wait_ge(s["weff"].h, wv)
            last = None
            for c in range(DC):
                last = e.transpose(ps_y[jj % 2][:, c * 128:(c + 1) * 128],
                                   wfst[jj % 2][:, c * 128:(c + 1) * 128],
                                   idf[:, :])
            last.then_inc(s["wft"].h, 1)
        s["wft"].inc()

        @P.on("vector")
        def _(e, jj=jj, w=s["wft"].n, wd=s["dshld"].n):
            e.wait_ge(s["wft"].h, w)
            e.wait_ge(s["dshld"].h, wd)
            for c in range(DC):
                ov = weff[:, c * 1024 + jj * 128: c * 1024 + (jj + 1) * 128]
                e.tensor_tensor(ov, ps_y[jj % 2][:, c * 128:(c + 1) * 128],
                                dsh2[:, jj * 128:(jj + 1) * 128], ALU.add)
            e.nop().then_inc(s["weff"].h, 1)
        s["weff"].inc()

    for c in range(DC):
        stg = wost[c % 2]

        smo = "wo_ld" if c % 2 == 0 else "wo_ldB"

        @P.on("sync")
        def _(e, c=c, stg=stg, smo=smo, wfree=s["wo_c"].n - 1):
            if wfree > 0:
                e.wait_ge(s["wo_c"].h, wfree)
            e.dma_start(out=stg[:, :], in_=d_wo[c * 128:(c + 1) * 128, :]).then_inc(s[smo].h, 16)
        s[smo].inc(16)

        @P.on("vector")
        def _(e, c=c, stg=stg, smo=smo, w=s[smo].n):
            e.wait_ge(s[smo].h, w)
            e.tensor_copy(wo_bf[:, c * 1024:(c + 1) * 1024], stg[:, :]).then_inc(s["wo_c"].h, 1)
        s["wo_c"].inc()

    @P.on("gpsimd")
    def _(e, w=s["cc"].n, wv=s["weff"].n, wl=16 * 6):
        e.wait_ge(s["wld"].h, wl)
        e.wait_ge(s["cc"].h, w)
        e.wait_ge(s["weff"].h, wv)
        e.wait_ge(s["wo_c"].h, DC)
        with e.register("gb0") as r0, e.register("gb1") as r1:
            e.reg_load(r0, gbt[0:1, 0:1])
            for c in range(DC):
                e.reg_mov(r1, c * 128 * TQ)
                e.reg_add(r1, r1, r0)
                e.dma_start(out=ao[:, c * TQ:(c + 1) * TQ],
                            in_=bass.AP(t_a2o, r1, [[TQ, 128], [1, TQ]])).then_inc(s["aold"].h, 16)
    s["aold"].inc(16 * DC)

    def p4_row(i):
        j = i % 2

        smr = "xr" if j == 0 else "xrB"

        @P.on("sync")
        def _(e, i=i, j=j, smr=smr, wfree=s["x1"].n - 1):
            if i >= 2:
                e.wait_ge(s["x1"].h, wfree)
            e.dma_start(out=xrs[j][:, :], in_=d_xres[i * 128:(i + 1) * 128, :]).then_inc(s[smr].h, 16)
        s[smr].inc(16)

        @P.on("tensor")
        def _(e, i=i, j=j, w=s["aold"].n, wfree=s["x1"].n - 1):
            e.wait_ge(s["aold"].h, w)
            if i >= 2:
                e.wait_ge(s["x1"].h, wfree)
            last = None
            for c in range(DC):
                for bl in range(2):
                    last = e.matmul(ps_y[j][:, bl * 512:(bl + 1) * 512],
                                    ao[:, c * TQ + i * 128: c * TQ + (i + 1) * 128],
                                    wo_bf[:, c * 1024 + bl * 512: c * 1024 + (bl + 1) * 512],
                                    start=(c == 0), stop=(c == DC - 1))
            last.then_inc(s["ymm"].h, 1)
        s["ymm"].inc()

        sqw = [("ymm", s["ymm"].n)] + ([("x1", s["x1"].n)] if i >= 1 else [])
        emit_ln(P, s, "sqx", ps_y[j][:, :], ysq2[:, :], lsm2, epsT, sqw, [],
                chain_waits=([("x1", s["x1"].n)] if i >= 1 else []))

        @P.on("vector")
        def _(e, i=i, j=j, smr=smr, w=s["sqx"].n, wx=s[smr].n):
            e.wait_ge(s["sqx"].h, w)
            e.wait_ge(s[smr].h, wx)
            e.tensor_scalar(ysq2[:, :], ps_y[j][:, :], lsm2[:, 9:10], lsm2[:, 8:9],
                            ALU.add, ALU.mult)
            e.tensor_tensor(x1[:, i * D:(i + 1) * D], xrs[j][:, :], ysq2[:, :],
                            ALU.add).then_inc(s["x1"].h, 1)
        s["x1"].inc()

    def p5_row(i):
        j = i % 2

        sqw = [("x1", i + 1)] + ([("xh2", s["xh2"].n - 2)] if i >= 1 else [])
        emit_ln(P, s, "xh2", x1[:, i * D:(i + 1) * D], ysq3[:, :], lsm3, epsT, sqw, [])

        @P.on("vector")
        def _(e, i=i, j=j, w=s["xh2"].n, wt=s["tp2"].n - 1):
            e.wait_ge(s["xh2"].h, w)
            if i >= 2:
                e.wait_ge(s["tp2"].h, wt)
            e.tensor_scalar(xh2[j][:, :], x1[:, i * D:(i + 1) * D], lsm3[:, 9:10],
                            lsm3[:, 8:9], ALU.add, ALU.mult).then_inc(s["xh2"].h, 1)
        s["xh2"].inc()

        @P.on("tensor")
        def _(e, i=i, j=j, w=s["xh2"].n, wfree=s["xh2T"].n - 1):
            e.wait_ge(s["xh2"].h, w)
            if i >= 2:
                e.wait_ge(s["xh2T"].h, wfree)
            last = None
            for c in range(DC):
                last = e.transpose(ps_t2[j][:, c * 128:(c + 1) * 128],
                                   xh2[j][:, c * 128:(c + 1) * 128], idb[:, :])
            last.then_inc(s["tp2"].h, 1)
        s["tp2"].inc()

        @P.on("scalar")
        def _(e, j=j, w=s["tp2"].n):
            e.wait_ge(s["tp2"].h, w)
            e.copy(xh2T[j][:, :], ps_t2[j][:, :]).then_inc(s["xh2T"].h, 1)
        s["xh2T"].inc()

        @P.on("tensor")
        def _(e, i=i, j=j, w=s["xh2T"].n, wfree=s["outp"].n):
            e.wait_ge(s["xh2T"].h, w)
            if i >= 1:
                e.wait_ge(s["outp"].h, wfree)
            last = None
            for c in range(DC):
                for bl in range(2):
                    last = e.matmul(ps_y5[:, bl * 512:(bl + 1) * 512],
                                    xh2T[j][:, c * 128:(c + 1) * 128],
                                    weff[:, c * 1024 + bl * 512: c * 1024 + (bl + 1) * 512],
                                    start=(c == 0), stop=(c == DC - 1))
            last.then_inc(s["y2"].h, 1)
        s["y2"].inc()

        sqw = [("y2", s["y2"].n)] + ([("outp", s["outp"].n)] if i >= 1 else [])
        emit_ln(P, s, "y2", ps_y5[:, :], ysq4[:, :], lsm4, epsT, sqw, [],
                chain_waits=([("outp", s["outp"].n)] if i >= 1 else []))

        smd = "outd" if j == 0 else "outdB"

        @P.on("vector")
        def _(e, i=i, j=j, smd=smd, w=s["y2"].n, wd=s["outd" if j == 0 else "outdB"].n):
            e.wait_ge(s["y2"].h, w)
            if i >= 2:
                e.wait_ge(s[smd].h, wd)
            e.tensor_scalar(ysq4[:, :], ps_y5[:, :], lsm4[:, 9:10], lsm4[:, 8:9],
                            ALU.add, ALU.mult)
            e.tensor_tensor(outt[j][:, :], x1[:, i * D:(i + 1) * D], ysq4[:, :],
                            ALU.add).then_inc(s["outp"].h, 1)
        s["outp"].inc()

        @P.on("gpsimd")
        def _(e, i=i, j=j, smd=smd, w=s["outp"].n):
            e.wait_ge(s["outp"].h, w)
            e.dma_start(out=d_out[i * 128:(i + 1) * 128, :], in_=outt[j][:, :]).then_inc(s[smd].h, 16)
        s[smd].inc(16)

    for i in range(NTQ):
        p4_row(i)
        if i >= 1:
            p5_row(i - 1)
    p5_row(NTQ - 1)

    if dbg:
        @P.on("gpsimd")
        def _(e, wa=s["aold"].n, wx=s["x1"].n, ww=s["weff"].n):
            e.wait_ge(s["aold"].h, wa)
            e.wait_ge(s["x1"].h, wx)
            e.wait_ge(s["weff"].h, ww)
            e.dma_start(out=d_oao[:, :], in_=ao[:, :]).then_inc(s["outd"].h, 16)
            e.dma_start(out=d_owob[:, :], in_=wo_bf[:, :]).then_inc(s["outd"].h, 16)
            e.dma_start(out=d_ox1[:, :], in_=x1[:, :]).then_inc(s["outd"].h, 16)
            e.dma_start(out=d_oweff[:, :], in_=weff[:, :]).then_inc(s["outd"].h, 16)
        s["outd"].inc(64)

    @P.on("gpsimd")
    def _(e, w=s["outd"].n, wb=s["outdB"].n):
        e.wait_ge(s["outd"].h, w)
        if wb > 0:
            e.wait_ge(s["outdB"].h, wb)

    P.flush()
    block_cm.__exit__(None, None, None)
    es4.close()
    es_all.close()
    return nc


def _make_aux(cfg):
    S, NT = cfg["S"], cfg["NT"]
    g = np.asarray(RIEMANN, dtype=np.float32)[:32]
    inv = (np.float32(1.0) / (g / g[0])).astype(np.float32)
    pos = np.arange(S, dtype=np.float32)
    th = pos[:, None] * inv[None, :]
    th0 = np.ascontiguousarray(th.reshape(NT, 128, 32).transpose(1, 0, 2).reshape(128, NT * 32))
    idb = np.eye(128, dtype=ml_dtypes.bfloat16)
    idf = np.eye(128, dtype=np.float32)
    mkz = np.zeros((128, 512), dtype=np.float32)
    mkz[:, 384:512] = (np.arange(128)[None, :] >= np.arange(128)[:, None])
    mkz = mkz.astype(ml_dtypes.bfloat16)
    one = np.ones((1, 128), dtype=np.float32)
    return th0, idb, idf, mkz, one


def kernel(L0_attn_x, L0_attn_w_qkv, L0_attn_w_o, L0_attn_log_scale,
           L0_ffn_weight, L0_ffn_disha, _cfg=None, _ret_nc=False, _dbg=False):
    cfg = _cfg or make_cfg()
    S, TQ = cfg["S"], cfg["TQ"]
    th0, idb, idf, mkz, one = _make_aux(cfg)
    x = np.asarray(L0_attn_x, dtype=np.float32)
    wqkv_f = np.asarray(L0_attn_w_qkv, dtype=np.float32)
    wo = np.ascontiguousarray(np.asarray(L0_attn_w_o, dtype=np.float32))
    lsc = np.asarray(L0_attn_log_scale, dtype=np.float32)
    wf = np.ascontiguousarray(np.asarray(L0_ffn_weight, dtype=np.float32))
    dsh = np.ascontiguousarray(np.asarray(L0_ffn_disha, dtype=np.float32))

    in_maps = []
    for c in range(8):
        B, R = c // 4, c % 4
        H0 = 4 * R
        in_maps.append({
            "x": np.ascontiguousarray(x[B]),
            "xres": np.ascontiguousarray(x[B, R * TQ:(R + 1) * TQ]),
            "wq": np.ascontiguousarray(wqkv_f[:, H0 * 64:H0 * 64 + 256]),
            "wk": np.ascontiguousarray(wqkv_f[:, 1024 + H0 * 64:1024 + H0 * 64 + 256]),
            "wv": np.ascontiguousarray(wqkv_f[:, 2048 + H0 * 64:2048 + H0 * 64 + 256]),
            "wo": wo, "wf": wf, "dsh": dsh,
            "lsc": np.ascontiguousarray(lsc[H0:H0 + 4].reshape(1, 4)),
            "th0": th0, "idb": idb, "idf": idf, "mkz": mkz, "one": one,
            "gb": np.array([[B * 4 * 256 * TQ]], dtype=np.int32),
        })
    nc = build_program(cfg, dbg=_dbg)
    if _ret_nc:
        return nc, in_maps
    res = run_bass_kernel_spmd(nc, in_maps, core_ids=list(range(8)))
    out = np.empty((2, S, 1024), dtype=np.float32)
    for c in range(8):
        B, R = c // 4, c % 4
        out[B, R * TQ:(R + 1) * TQ] = res.results[c]["out"]
    if _dbg:
        return out, res.results
    return out


# version 08
# speedup vs baseline: 1.0034x; 1.0034x over previous
from contextlib import ExitStack

import numpy as np
import ml_dtypes

import concourse.bass as bass
import concourse.mybir as mybir
from concourse.bass_utils import run_bass_kernel_spmd
from concourse.aot_env import allow_ici_all_to_all

F32 = mybir.dt.float32
BF16 = mybir.dt.bfloat16
I32 = mybir.dt.int32
AF = mybir.ActivationFunctionType
ALU = mybir.AluOpType
AX = mybir.AxisListType.X

PI = float(np.pi)
PI_IN = float(np.nextafter(np.float32(PI), np.float32(0.0)))
INV2PI = float(np.float32(1.0 / (2 * np.pi)))
C1 = 6.28125
_r1 = 2 * np.pi - C1
C2 = float(np.float32(np.ldexp(np.round(np.ldexp(_r1, 19)), -19)))
C3 = float(np.float32(_r1 - C2))
C23 = float(np.float32(C2 + C3))
EPS = 1e-5

RIEMANN = (14.134725142, 21.022039639, 25.01085758, 30.424876126, 32.935061588,
           37.586178159, 40.918719012, 43.327073281, 48.005150881, 49.773832478,
           52.970321478, 56.446247697, 59.347044003, 60.831778525, 65.112544048,
           67.079810529, 69.546401711, 72.067157674, 75.704690699, 77.144840069,
           79.33737502, 82.910380854, 84.735492981, 87.425274613, 88.809111208,
           92.491899271, 94.651344041, 95.870634228, 98.831194218, 101.317851006,
           103.72553804, 105.446623052)


def make_cfg(S=8192, QC=1024):
    return dict(S=S, D=1024, QC=QC, NT=S // 128, NQC=S // QC, NB=QC // 512,
                NH=4, DH=64, DC=8, TQ=S // 4, NTQ=S // 4 // 128)


class Prog:

    ENGS = ("sync", "gpsimd", "vector", "scalar", "tensor")

    def __init__(self, block):
        self.block = block
        self.buf = {k: [] for k in self.ENGS}

    def on(self, eng):
        def deco(fn):
            self.buf[eng].append(fn)
            return None
        return deco

    def flush(self):
        for eng in self.ENGS:
            fns = self.buf[eng]
            if not fns:
                continue

            def section(e, fns=list(fns)):
                for f in fns:
                    f(e)
            getattr(self.block, eng)(section)
            self.buf[eng] = []


class Sem:
    def __init__(self, nc, es, name):
        self.h = es.enter_context(nc.semaphore(name))
        self.n = 0

    def inc(self, k=1):
        self.n += k
        return self.n



def emit_ln(P, s, sem_name, x_ap, xsq, lsm, epsT, sq_waits, red_waits, chain_waits=()):
    import concourse.mybir as _mb
    D = 1024

    @P.on("scalar")
    def _(e, waits=tuple(sq_waits)):
        for (ss, v) in waits:
            e.wait_ge(s[ss].h, v)
        e.activation(xsq, x_ap, AF.Square).then_inc(s[sem_name].h, 1)
    s[sem_name].inc()

    @P.on("vector")
    def _(e, w=s[sem_name].n, waits=tuple(red_waits)):
        for (ss, v) in waits:
            e.wait_ge(s[ss].h, v)
        e.wait_ge(s[sem_name].h, w)
        e.reduce_sum(lsm[:, 0:1], x_ap, axis=AX)
        e.reduce_sum(lsm[:, 1:2], xsq, axis=AX).then_inc(s[sem_name].h, 1)
    s[sem_name].inc()

    @P.on("scalar")
    def _(e, w=s[sem_name].n, waits=tuple(chain_waits)):
        for (ss, v) in waits:
            e.wait_ge(s[ss].h, v)
        e.wait_ge(s[sem_name].h, w)
        e.activation(lsm[:, 3:4], lsm[:, 0:1], AF.Square, scale=1.0 / D)
        e.activation(lsm[:, 4:5], lsm[:, 3:4], AF.Identity, scale=-1.0)
        e.activation(lsm[:, 2:3], lsm[:, 0:1], AF.Identity, scale=1.0 / D)
        e.activation(lsm[:, 9:10], lsm[:, 2:3], AF.Identity, scale=-1.0)
        e.activation(lsm[:, 5:6], lsm[:, 1:2], AF.Identity, scale=1.0 / D,
                     bias=lsm[:, 4:5])
        e.activation(lsm[:, 6:7], lsm[:, 5:6], AF.Identity, bias=epsT[:, 0:1])
        e.activation(lsm[:, 7:8], lsm[:, 6:7], AF.Ln)
        e.activation(lsm[:, 8:9], lsm[:, 7:8], AF.Exp, scale=-0.5).then_inc(s[sem_name].h, 1)
    s[sem_name].inc()


def build_program(cfg, dbg=False):
    S, D, QC = cfg["S"], cfg["D"], cfg["QC"]
    NT, NQC, NB = cfg["NT"], cfg["NQC"], cfg["NB"]
    NH, DH, DC = cfg["NH"], cfg["DH"], cfg["DC"]
    TQ, NTQ = cfg["TQ"], cfg["NTQ"]
    KPC = QC // 128

    nc = bass.Bass("TRN2", target_bir_lowering=False)

    d_x = nc.dram_tensor("x", [S, D], F32, kind="ExternalInput").ap()
    d_xres = nc.dram_tensor("xres", [TQ, D], F32, kind="ExternalInput").ap()
    d_wq = nc.dram_tensor("wq", [D, NH * DH], F32, kind="ExternalInput").ap()
    d_wk = nc.dram_tensor("wk", [D, NH * DH], F32, kind="ExternalInput").ap()
    d_wv = nc.dram_tensor("wv", [D, NH * DH], F32, kind="ExternalInput").ap()
    d_wo = nc.dram_tensor("wo", [D, D], F32, kind="ExternalInput").ap()
    d_wf = nc.dram_tensor("wf", [D, D], F32, kind="ExternalInput").ap()
    d_dsh = nc.dram_tensor("dsh", [64, D], F32, kind="ExternalInput").ap()
    d_lsc = nc.dram_tensor("lsc", [1, NH], F32, kind="ExternalInput").ap()
    d_th0 = nc.dram_tensor("th0", [128, NT * 32], F32, kind="ExternalInput").ap()
    d_idb = nc.dram_tensor("idb", [128, 128], BF16, kind="ExternalInput").ap()
    d_idf = nc.dram_tensor("idf", [128, 128], F32, kind="ExternalInput").ap()
    d_mkz = nc.dram_tensor("mkz", [128, 512], BF16, kind="ExternalInput").ap()
    d_one = nc.dram_tensor("one", [1, 128], F32, kind="ExternalInput").ap()
    d_gb = nc.dram_tensor("gb", [1, 1], I32, kind="ExternalInput").ap()
    d_out = nc.dram_tensor("out", [TQ, D], F32, kind="ExternalOutput").ap()

    t_a2i = nc.dram_tensor("a2i", [8 * NH * 64, TQ], BF16)
    t_a2o = nc.dram_tensor("a2o", [8 * NH * 64, TQ], BF16)
    if dbg:
        d_okfm = nc.dram_tensor("o_kfm", [64, NH * S], BF16, kind="ExternalOutput").ap()
        d_ovx = nc.dram_tensor("o_vx", [128, NT * NH * 65], BF16, kind="ExternalOutput").ap()
        d_onegm = nc.dram_tensor("o_negm", [128, 2 * NH], F32, kind="ExternalOutput").ap()

    es_all = ExitStack()
    E = es_all.enter_context

    s = {}
    for nm in ["x", "th", "ln", "sqx", "tpx", "xhT", "qkv", "trig", "sin",
               "rope", "sq", "nrm", "ktr", "kev", "bar", "wld", "wcast",
               "qtr", "qev", "sc", "exp", "msk", "pv", "dnm", "rcp", "bc",
               "rb", "ao", "aodma", "cc", "aold", "wfld", "wft", "weff",
               "xr", "ymm", "x1", "xh2", "tp2", "xh2T", "y2", "outp",
               "outd", "wo_ld", "wo_c", "dshld", "msc",
               "xB", "thB", "wkld", "wkldB", "wo_ldB", "wfldB", "xrB", "outdB",
               "vmm", "vev", "nfz"]:
        s[nm] = Sem(nc, es_all, "s_" + nm)

    scbc = E(nc.sbuf_tensor("scbc", [128, NH], F32))
    negM = E(nc.sbuf_tensor("negM", [128, 2 * NH], F32))
    nrm = [E(nc.sbuf_tensor(f"nrm{j}", [128, 8], F32)) for j in range(2)]
    epsT = E(nc.sbuf_tensor("epsT", [128, 1], F32))
    idb = E(nc.sbuf_tensor("sb_idb", [128, 128], BF16))
    idf = E(nc.sbuf_tensor("sb_idf", [128, 128], F32))
    mkz = E(nc.sbuf_tensor("sb_mkz", [128, 512], BF16))
    onef = E(nc.sbuf_tensor("onef", [1, 128], F32))
    lscT = E(nc.sbuf_tensor("lscT", [1, NH], F32))
    gbt = E(nc.sbuf_tensor("gbt", [1, 1], I32))

    esM = ExitStack()
    EM = esM.enter_context
    Kfm = EM(nc.sbuf_tensor("Kfm", [64, NH * S], BF16))
    Vx = EM(nc.sbuf_tensor("Vx", [128, NT * NH * 65], BF16))
    qall = [EM(nc.sbuf_tensor(f"qall{j}", [64, NH * QC], BF16)) for j in range(2)]
    wqkv = EM(nc.sbuf_tensor("wqkv", [128, DC * 768], BF16))
    xstg = [EM(nc.sbuf_tensor(f"xstg{j}", [128, D], F32)) for j in range(2)]
    xsq = EM(nc.sbuf_tensor("xsq", [128, D], F32))
    xhat = [EM(nc.sbuf_tensor(f"xhat{j}", [128, D], BF16)) for j in range(2)]
    xhT = [EM(nc.sbuf_tensor(f"xhT{j}", [128, D], BF16)) for j in range(2)]
    Qtm = [EM(nc.sbuf_tensor(f"Qtm{j}", [128, 256], BF16)) for j in range(2)]
    krot = [EM(nc.sbuf_tensor(f"krot{j}", [128, 256], BF16)) for j in range(2)]
    TB = 8
    NBATCH = (NT + TB - 1) // TB
    th0b = [EM(nc.sbuf_tensor(f"th0b{j}", [128, TB * 32], F32)) for j in range(2)]
    cosr = [EM(nc.sbuf_tensor(f"cosr{j}", [128, TB * 128], BF16)) for j in range(2)]
    sinr = [EM(nc.sbuf_tensor(f"sinr{j}", [128, TB * 128], BF16)) for j in range(2)]
    thv = EM(nc.sbuf_tensor("thv", [128, 256], F32))
    tki = EM(nc.sbuf_tensor("tki", [128, 256], I32))
    tkf = EM(nc.sbuf_tensor("tkf", [128, 256], F32))
    tg1 = EM(nc.sbuf_tensor("tg1", [128, 256], F32))
    tg2 = EM(nc.sbuf_tensor("tg2", [128, 256], F32))
    tg3 = EM(nc.sbuf_tensor("tg3", [128, 256], F32))
    tu1 = EM(nc.sbuf_tensor("tu1", [128, 256], F32))
    tu2 = EM(nc.sbuf_tensor("tu2", [128, 256], F32))
    trs = [EM(nc.sbuf_tensor(f"trs{j}", [128, 256], F32)) for j in range(2)]
    trc = [EM(nc.sbuf_tensor(f"trc{j}", [128, 256], F32)) for j in range(2)]
    tmpa = EM(nc.sbuf_tensor("tmpa", [128, 128], F32))
    tmpb = EM(nc.sbuf_tensor("tmpb", [128, 128], F32))
    sqs = [EM(nc.sbuf_tensor(f"sqs{j}", [128, 512], F32)) for j in range(2)]
    lsm = EM(nc.sbuf_tensor("lsm", [128, 16], F32))
    nrw = [EM(nc.sbuf_tensor(f"nrw{j}", [128, 8], F32)) for j in range(2)]
    nfin = EM(nc.sbuf_tensor("nfin", [128, 8], F32))
    nfs = EM(nc.sbuf_tensor("nfs", [128, 512], F32))
    pt = [EM(nc.sbuf_tensor(f"pt{j}", [128, QC], BF16)) for j in range(3)]
    aos = [EM(nc.sbuf_tensor(f"aos{j}", [64, QC], BF16)) for j in range(2)]
    dnm = EM(nc.sbuf_tensor("dnm", [1, QC], F32))
    rcpr = EM(nc.sbuf_tensor("rcpr", [1, 2 * QC], F32))
    rbt = EM(nc.sbuf_tensor("rbt", [64, 512], F32))
    ptm = EM(nc.sbuf_tensor("ptm", [128, 512], BF16))
    ps_sc = [EM(nc.psum_tensor(f"ps_sc{j}", [128, QC], F32)) for j in range(2)]
    ps_pv = EM(nc.psum_tensor("ps_pv", [65, QC], F32))
    ps_f32 = EM(nc.psum_tensor("ps_f32", [128, 512], F32))
    ps_bf = EM(nc.psum_tensor("ps_bf", [128, D], BF16))

    block_cm = nc.Block()
    block = block_cm.__enter__()
    P = Prog(block)

    es0 = ExitStack()
    E0 = es0.enter_context
    wstg = [E0(nc.sbuf_tensor("wstg0", [128, 768], F32))]

    @P.on("sync")
    def _(e):
        for dst, src in ((idb, d_idb), (idf, d_idf), (mkz, d_mkz),
                         (onef, d_one), (lscT, d_lsc), (gbt, d_gb)):
            e.dma_start(out=dst[:, :], in_=src[:, :]).then_inc(s["wld"].h, 16)
    s["wld"].inc(96)

    @P.on("scalar")
    def _(e, w=s["wld"].n):
        e.wait_ge(s["wld"].h, w)
        e.activation(lscT[:, :], lscT[:, :], AF.Exp).then_inc(s["msc"].h, 1)
    s["msc"].inc()

    @P.on("tensor")
    def _(e, w=s["msc"].n):
        e.wait_ge(s["msc"].h, w)
        e.matmul(ps_f32[:, 0:NH], onef[0:1, :], lscT[0:1, :], start=True,
                 stop=True).then_inc(s["msc"].h, 1)
    s["msc"].inc()

    @P.on("vector")
    def _(e, w=s["msc"].n):
        e.wait_ge(s["msc"].h, w)
        e.tensor_copy(scbc[:, :], ps_f32[:, 0:NH])
        e.memset(bass.AP(Vx, 64, [[NT * NH * 65, 128], [65, NT * NH], [1, 1]]), 1.0)
        e.memset(nrm[0][:, :], 0.0)
        e.memset(nrm[1][:, :], 0.0)
        e.memset(epsT[:, :], EPS).then_inc(s["msc"].h, 1)
    s["msc"].inc()
    MSC_ALL = s["msc"].n

    for c in range(DC):
        stg = wstg[0]

        sm = "wkld" if c % 2 == 0 else "wkldB"

        @P.on("sync")
        def _(e, c=c, stg=stg, sm=sm, wfree=s["wcast"].n):
            if wfree > 0:
                e.wait_ge(s["wcast"].h, wfree)
            e.dma_start(out=stg[:, 0:256], in_=d_wq[c * 128:(c + 1) * 128, :]).then_inc(s[sm].h, 16)
            e.dma_start(out=stg[:, 256:512], in_=d_wk[c * 128:(c + 1) * 128, :]).then_inc(s[sm].h, 16)
            e.dma_start(out=stg[:, 512:768], in_=d_wv[c * 128:(c + 1) * 128, :]).then_inc(s[sm].h, 16)
        s[sm].inc(48)

        @P.on("vector")
        def _(e, c=c, stg=stg, sm=sm, w=s[sm].n):
            e.wait_ge(s[sm].h, w)
            e.tensor_copy(wqkv[:, c * 768:(c + 1) * 768], stg[:, :]).then_inc(s["wcast"].h, 1)
        s["wcast"].inc()

    def trig_nsub(b):
        ntl = min(TB, NT - b * TB)
        return (ntl * 128 + 255) // 256

    def emit_trig_sub(b, sub):
        ntl = min(TB, NT - b * TB)
        smt = "th" if b % 2 == 0 else "thB"
        if sub == 0:
            @P.on("sync")
            def _(e, b=b, ntl=ntl, smt=smt, wtr=s["trig"].n):
                if b >= 2:
                    e.wait_ge(s["trig"].h, wtr)
                e.dma_start(out=th0b[b % 2][:, 0:ntl * 32],
                            in_=d_th0[:, b * TB * 32:(b * TB + ntl) * 32]).then_inc(s[smt].h, 16)
            s[smt].inc(16)
        ncols = min(256, ntl * 128 - sub * 256)

        @P.on("vector")
        def _(e, b=b, sub=sub, ncols=ncols, smt=smt, wth=s[smt].n, wm=s["msc"].n,
              wtr=s["trig"].n - 1):
            e.wait_ge(s[smt].h, wth)
            e.wait_ge(s["msc"].h, wm)
            if wtr > 0:
                e.wait_ge(s["trig"].h, wtr)
            if b >= 2:
                e.wait_ge(s["rope"].h, (b - 1) * TB)
            ntile4 = ncols // 128
            for h in range(NH):
                ov = bass.AP(thv, h * 32, [[256, 128], [128, ntile4], [1, 32]])
                iv = bass.AP(th0b[b % 2], sub * 64, [[TB * 32, 128], [32, ntile4], [1, 32]])
                e.tensor_scalar(ov, iv, scbc[:, h:h + 1], None, ALU.mult)
            c = slice(0, ncols)
            e.tensor_scalar(tu1[:, c], thv[:, c], INV2PI, None, ALU.mult)
            e.tensor_copy(tki[:, c], tu1[:, c])
            e.tensor_copy(tkf[:, c], tki[:, c])
            e.scalar_tensor_tensor(tu1[:, c], tkf[:, c], -C1, thv[:, c], ALU.mult, ALU.add)
            e.scalar_tensor_tensor(tu2[:, c], tkf[:, c], -C2, tu1[:, c], ALU.mult, ALU.add)
            e.scalar_tensor_tensor(tg1[:, c], tkf[:, c], -C3, tu2[:, c], ALU.mult, ALU.add)
            e.tensor_scalar(tg2[:, c], tg1[:, c], PI, None, ALU.is_gt)
            e.tensor_scalar(tg3[:, c], tg1[:, c], -PI, None, ALU.is_lt)
            e.tensor_tensor(tu1[:, c], tg3[:, c], tg2[:, c], ALU.subtract)
            e.scalar_tensor_tensor(tu2[:, c], tu1[:, c], C1, tg1[:, c], ALU.mult, ALU.add)
            e.scalar_tensor_tensor(tg3[:, c], tu1[:, c], C23, tu2[:, c], ALU.mult, ALU.add)
            e.tensor_scalar(trs[sub % 2][:, c], tg3[:, c], -PI_IN, PI_IN, ALU.max, ALU.min)
            e.tensor_scalar(tu1[:, c], trs[sub % 2][:, c], PI / 2, None, ALU.add)
            e.tensor_scalar(tg2[:, c], tu1[:, c], PI, None, ALU.is_gt)
            e.scalar_tensor_tensor(tu2[:, c], tg2[:, c], -C1, tu1[:, c], ALU.mult, ALU.add)
            e.scalar_tensor_tensor(tg3[:, c], tg2[:, c], -C23, tu2[:, c], ALU.mult, ALU.add)
            e.tensor_scalar(trc[sub % 2][:, c], tg3[:, c], -PI_IN, PI_IN,
                            ALU.max, ALU.min).then_inc(s["sin"].h, 1)
        s["sin"].inc()

        @P.on("scalar")
        def _(e, b=b, sub=sub, ncols=ncols, w=s["sin"].n):
            e.wait_ge(s["sin"].h, w)
            e.activation(sinr[b % 2][:, sub * 256:sub * 256 + ncols],
                         trs[sub % 2][:, 0:ncols], AF.Sin)
            e.activation(cosr[b % 2][:, sub * 256:sub * 256 + ncols],
                         trc[sub % 2][:, 0:ncols], AF.Sin).then_inc(s["trig"].h, 1)
        s["trig"].inc()

    pref_qev, pref_kev, pref_vev, pref_nfz = {}, {}, {}, {}
    pref_sc2, pref_exp = {}, {}
    chst = dict(pend_tail=[], pend_fin=None, sc_free=[0, 0], pt_free=[0, 0, 0])
    PVLAG = 2

    def tile_gen(i):
        j = i % 2
        bnext = i // TB + 1
        if bnext < NBATCH and i % TB >= TB // 2:
            _sub = i % TB - TB // 2
            if _sub < trig_nsub(bnext):
                emit_trig_sub(bnext, _sub)

        smx = "x" if j == 0 else "xB"

        @P.on("gpsimd")
        def _(e, i=i, j=j, smx=smx):
            if i >= 2:
                e.wait_ge(s["ln"].h, i - 1)
            e.dma_start(out=xstg[j][:, :], in_=d_x[i * 128:(i + 1) * 128, :]).then_inc(s[smx].h, 16)
        s[smx].inc(16)

        sq_waits = [(smx, s[smx].n)] + ([("sqx", s["sqx"].n)] if i >= 1 else [])
        red_waits = [("tpx", s["tpx"].n - 1)] if i >= 2 else []
        emit_ln(P, s, "sqx", xstg[j][:, :], xsq[:, :], lsm, epsT, sq_waits, red_waits,
                chain_waits=([("ln", i)] if i >= 1 else []))

        @P.on("vector")
        def _(e, j=j, w=s["sqx"].n):
            e.wait_ge(s["sqx"].h, w)
            e.tensor_scalar(xhat[j][:, :], xstg[j][:, :], lsm[:, 9:10], lsm[:, 8:9],
                            ALU.add, ALU.mult).then_inc(s["ln"].h, 1)
        s["ln"].inc()

        if i == 0:
            for _sub in range(trig_nsub(0)):
                emit_trig_sub(0, _sub)

        yield

        @P.on("tensor")
        def _(e, i=i, j=j, wl=s["ln"].n, wx=s["xhT"].n, wk=s["kev"].n, wq=s["qev"].n):
            e.wait_ge(s["ln"].h, wl)
            if i >= 1:
                e.wait_ge(s["xhT"].h, wx)
                if wk > 0:
                    e.wait_ge(s["kev"].h, wk)
                if wq > 0:
                    e.wait_ge(s["qev"].h, wq)
            last = None
            for c in range(DC):
                last = e.transpose(ps_bf[:, c * 128:(c + 1) * 128],
                                   xhat[j][:, c * 128:(c + 1) * 128], idb[:, :])
            last.then_inc(s["tpx"].h, 1)
        s["tpx"].inc()

        @P.on("scalar")
        def _(e, j=j, w=s["tpx"].n):
            e.wait_ge(s["tpx"].h, w)
            e.copy(xhT[j][:, :], ps_bf[:, :]).then_inc(s["xhT"].h, 1)
        s["xhT"].inc()

        @P.on("tensor")
        def _(e, i=i, j=j, wx=s["xhT"].n, wr=s["rope"].n, wv=s["vev"].n,
              wrb=s["rb"].n, wnf=s["nfz"].n, wc=s["wcast"].n, wm=MSC_ALL):
            e.wait_ge(s["xhT"].h, wx)
            if wr > 0:
                e.wait_ge(s["rope"].h, wr)
            if wv > 0:
                e.wait_ge(s["vev"].h, wv)
            if wrb > 0:
                e.wait_ge(s["rb"].h, wrb)
            if wnf > 0:
                e.wait_ge(s["nfz"].h, wnf)
            if i == 0:
                e.wait_ge(s["msc"].h, wm)
            if i >= 2:
                e.wait_ge(s["wcast"].h, wc)
            last = None
            for c in range(DC):
                if i < 2:
                    e.wait_ge(s["wcast"].h, c + 1)
                st, sp = (c == 0), (c == DC - 1)
                last = e.matmul(ps_f32[:, 0:512], xhT[j][:, c * 128:(c + 1) * 128],
                                wqkv[:, c * 768:c * 768 + 512], start=st, stop=sp)
            last.then_inc(s["qkv"].h, 1)
        s["qkv"].inc()

        yield

        @P.on("vector")
        def _(e, i=i, j=j, wq_=s["qkv"].n, ws=s["trig"].n, wk=s["ktr"].n,
              wqt=s["qtr"].n, wsq=s["sq"].n):
            e.wait_ge(s["qkv"].h, wq_)
            e.wait_ge(s["trig"].h, ws)
            if i >= 2:
                e.wait_ge(s["ktr"].h, wk - 1)
                e.wait_ge(s["qtr"].h, wqt - 1)
                e.wait_ge(s["sq"].h, wsq - 1)
            rb, ro = (i // TB) % 2, (i % TB) * 128
            cosv = bass.AP(cosr[rb], ro, [[TB * 128, 128], [32, NH], [1, 32]])
            sinv = bass.AP(sinr[rb], ro, [[TB * 128, 128], [32, NH], [1, 32]])
            last = None
            for (base, dst) in ((0, Qtm[j]), (256, krot[j])):
                x1 = bass.AP(ps_f32, base, [[512, 128], [64, NH], [2, 32]])
                x2 = bass.AP(ps_f32, base + 1, [[512, 128], [64, NH], [2, 32]])
                o1 = bass.AP(dst, 0, [[256, 128], [64, NH], [1, 32]])
                o2 = bass.AP(dst, 32, [[256, 128], [64, NH], [1, 32]])
                va = bass.AP(tmpa, 0, [[128, 128], [32, NH], [1, 32]])
                vb = bass.AP(tmpb, 0, [[128, 128], [32, NH], [1, 32]])
                e.tensor_tensor(va, x1, cosv, ALU.mult)
                e.tensor_tensor(vb, x2, sinv, ALU.mult)
                e.tensor_tensor(o1, va, vb, ALU.subtract)
                e.tensor_tensor(va, x1, sinv, ALU.mult)
                e.tensor_tensor(vb, x2, cosv, ALU.mult)
                last = e.tensor_tensor(o2, va, vb, ALU.add)
            last.then_inc(s["rope"].h, 1)
        s["rope"].inc()

        @P.on("tensor")
        def _(e, j=j, wr=s["rope"].n, wrb=s["rb"].n, wnf=s["nfz"].n):
            e.wait_ge(s["rope"].h, wr)
            if wrb > 0:
                e.wait_ge(s["rb"].h, wrb)
            if wnf > 0:
                e.wait_ge(s["nfz"].h, wnf)
            last = None
            for c in range(DC):
                st, sp = (c == 0), (c == DC - 1)
                last = e.matmul(ps_f32[:, 0:256], xhT[j][:, c * 128:(c + 1) * 128],
                                wqkv[:, c * 768 + 512:c * 768 + 768], start=st, stop=sp)
            last.then_inc(s["vmm"].h, 1)
        s["vmm"].inc()

        @P.on("vector")
        def _(e, i=i, w=s["vmm"].n):
            e.wait_ge(s["vmm"].h, w)
            vin = bass.AP(ps_f32, 0, [[512, 128], [64, NH], [1, 64]])
            vout = bass.AP(Vx, i * NH * 65, [[NT * NH * 65, 128], [65, NH], [1, 64]])
            e.tensor_copy(vout, vin).then_inc(s["vev"].h, 1)
        s["vev"].inc()

        @P.on("scalar")
        def _(e, i=i, j=j, w=s["rope"].n, wn=s["nrm"].n):
            e.wait_ge(s["rope"].h, w)
            if i >= 2:
                e.wait_ge(s["nrm"].h, wn - 1)
            e.activation(sqs[j][:, 0:256], Qtm[j][:, :], AF.Square)
            e.activation(sqs[j][:, 256:512], krot[j][:, :], AF.Square).then_inc(s["sq"].h, 1)
        s["sq"].inc()

        yield

        @P.on("vector")
        def _(e, i=i, j=j, w=s["sq"].n):
            e.wait_ge(s["sq"].h, w)
            v3 = bass.AP(sqs[j], 0, [[512, 128], [64, 8], [1, 64]])
            e.tensor_reduce(nrw[i % 2][:, :], v3, AX, ALU.add)
            if i >= 1:
                e.tensor_tensor(nrm[(i - 1) % 2][:, :], nrm[i % 2][:, :],
                                nrw[(i - 1) % 2][:, :], ALU.max)
            e.nop().then_inc(s["nrm"].h, 1)
        s["nrm"].inc()

        @P.on("tensor")
        def _(e, j=j, wr=s["rope"].n, wx=s["xhT"].n):
            e.wait_ge(s["rope"].h, wr)
            e.wait_ge(s["xhT"].h, wx)
            last = None
            for h in range(NH):
                last = e.transpose(ps_bf[0:64, h * 128:(h + 1) * 128],
                                   krot[j][:, h * 64:(h + 1) * 64], idb[:, :])
            last.then_inc(s["ktr"].h, 1)
        s["ktr"].inc()

        @P.on("vector")
        def _(e, i=i, w=s["ktr"].n):
            e.wait_ge(s["ktr"].h, w)
            kin = bass.AP(ps_bf, 0, [[1024, 64], [128, NH], [1, 128]])
            kout = bass.AP(Kfm, i * 128, [[NH * S, 64], [S, NH], [1, 128]])
            e.tensor_copy(kout, kin).then_inc(s["kev"].h, 1)
        s["kev"].inc()

        @P.on("tensor")
        def _(e, j=j, wr=s["rope"].n, wx=s["xhT"].n):
            e.wait_ge(s["rope"].h, wr)
            e.wait_ge(s["xhT"].h, wx)
            last = None
            for h in range(NH):
                last = e.transpose(ps_bf[0:64, 512 + h * 128:512 + (h + 1) * 128],
                                   Qtm[j][:, h * 64:(h + 1) * 64], idb[:, :])
            last.then_inc(s["qtr"].h, 1)
        s["qtr"].inc()

        @P.on("vector")
        def _(e, i=i, w=s["qtr"].n, wsc=pref_sc2.get(i // KPC - 2, 0)):
            e.wait_ge(s["qtr"].h, w)
            if wsc > 0:
                e.wait_ge(s["sc"].h, wsc)
            qin = bass.AP(ps_bf, 512, [[1024, 64], [128, NH], [1, 128]])
            qout = bass.AP(qall[(i // KPC) % 2], (i % KPC) * 128,
                           [[NH * QC, 64], [QC, NH], [1, 128]])
            e.tensor_copy(qout, qin).then_inc(s["qev"].h, 1)
        s["qev"].inc()

    def emit_finalize(q):
        t = (q + 1) * KPC - 1

        @P.on("vector")
        def _(e, t=t, w=s["nrm"].n):
            e.wait_ge(s["nrm"].h, w)
            e.tensor_tensor(nfin[:, :], nrm[(t - 1) % 2][:, :], nrw[t % 2][:, :],
                            ALU.max).then_inc(s["bar"].h, 1)
        s["bar"].inc()

        @P.on("tensor")
        def _(e, w=s["bar"].n, wv=s["vev"].n, wrb=s["rb"].n):
            e.wait_ge(s["bar"].h, w)
            e.wait_ge(s["vev"].h, wv)
            if wrb > 0:
                e.wait_ge(s["rb"].h, wrb)
            e.transpose(ps_f32[0:8, 0:128], nfin[:, 0:8], idf[:, :]).then_inc(s["bar"].h, 1)
        s["bar"].inc()

        @P.on("vector")
        def _(e, w=s["bar"].n):
            e.wait_ge(s["bar"].h, w)
            e.tensor_copy(nfs[0:8, 0:128], ps_f32[0:8, 0:128])
            e.tensor_copy(nfs[0:16, 256:384], idf[0:16, 0:128])
            e.reduce_max(nfs[0:8, 128:129], nfs[0:8, 0:128], axis=AX).then_inc(s["bar"].h, 1)
        s["bar"].inc()

        @P.on("scalar")
        def _(e, w=s["bar"].n):
            e.wait_ge(s["bar"].h, w)
            e.activation(nfs[0:8, 129:130], nfs[0:8, 128:129], AF.Sqrt).then_inc(s["bar"].h, 1)
        s["bar"].inc()

        @P.on("tensor")
        def _(e, w=s["bar"].n):
            e.wait_ge(s["bar"].h, w)
            e.transpose(ps_f32[0:1, 0:8], nfs[0:8, 129:130], idf[0:8, 0:8]).then_inc(s["bar"].h, 1)
        s["bar"].inc()

        @P.on("vector")
        def _(e, w=s["bar"].n):
            e.wait_ge(s["bar"].h, w)
            e.tensor_copy(nfs[0:1, 132:140], ps_f32[0:1, 0:8])
            e.tensor_copy(nfs[0:16, 256:384], idf[0:16, 0:128])
            e.tensor_copy(nfs[0:16, 384:512], idf[0:16, 0:128])
            e.tensor_tensor(nfs[0:1, 140:140 + NH], nfs[0:1, 132:132 + NH],
                            nfs[0:1, 136:136 + NH], ALU.mult)
            e.tensor_copy(nfs[0:16, 256:384], idf[0:16, 0:128])
            e.tensor_copy(nfs[0:16, 384:512], idf[0:16, 0:128])
            e.tensor_scalar(nfs[0:1, 148:148 + NH], nfs[0:1, 140:140 + NH],
                            -0.125, None, ALU.mult)
            e.tensor_copy(nfs[0:16, 256:384], idf[0:16, 0:128])
            e.tensor_copy(nfs[0:16, 384:512], idf[0:16, 0:128]).then_inc(s["bar"].h, 1)
        s["bar"].inc()

        @P.on("tensor")
        def _(e, w=s["bar"].n):
            e.wait_ge(s["bar"].h, w)
            e.matmul(ps_f32[:, 0:NH], onef[0:1, :], nfs[0:1, 148:148 + NH],
                     start=True, stop=True).then_inc(s["bar"].h, 1)
        s["bar"].inc()

        @P.on("vector")
        def _(e, q=q, w=s["bar"].n, we=pref_exp.get(q - 2, 0)):
            e.wait_ge(s["bar"].h, w)
            if we > 0:
                e.wait_ge(s["exp"].h, we)
            e.tensor_copy(negM[:, (q % 2) * NH:(q % 2) * NH + NH],
                          ps_f32[:, 0:NH]).then_inc(s["nfz"].h, 1)
        s["nfz"].inc()

        pref_qev[q] = s["qev"].n
        pref_kev[q] = s["kev"].n
        pref_vev[q] = s["vev"].n
        pref_nfz[q] = s["nfz"].n

    def emit_tail_piece(args):
        kind, h, qc, cpar, o = args
        if kind == "bc":
            @P.on("tensor")
            def _(e, o=o, w=s["rcp"].n, wrb=s["rb"].n, wv=s["vmm"].n,
                  wr=s["qkv"].n, wnf=s["nfz"].n):
                e.wait_ge(s["rcp"].h, w)
                if wrb > 0:
                    e.wait_ge(s["rb"].h, wrb)
                e.wait_ge(s["vev"].h, wv)
                e.wait_ge(s["rope"].h, wr)
                if wnf > 0:
                    e.wait_ge(s["nfz"].h, wnf)
                e.matmul(ps_f32[0:64, 0:512], onef[0:1, 0:64],
                         rcpr[0:1, QC + o:QC + o + 512],
                         start=True, stop=True).then_inc(s["bc"].h, 1)
            s["bc"].inc()

            @P.on("vector")
            def _(e, cpar=cpar, o=o, w=s["bc"].n, wdma=s["aodma"].n):
                e.wait_ge(s["bc"].h, w)
                if wdma > 0:
                    e.wait_ge(s["aodma"].h, wdma)
                e.tensor_copy(rbt[:, :], ps_f32[0:64, 0:512]).then_inc(s["rb"].h, 1)
                e.tensor_tensor(aos[cpar][:, o:o + 512], ps_pv[0:64, o:o + 512],
                                rbt[:, :], ALU.mult).then_inc(s["ao"].h, 1)
            s["rb"].inc()
            s["ao"].inc()
        else:
            @P.on("gpsimd")
            def _(e, h=h, qc=qc, cpar=cpar, w=s["ao"].n):
                e.wait_ge(s["ao"].h, w)
                quarter = qc // (NQC // 4)
                coff = (qc % (NQC // 4)) * QC
                for jj in (quarter, 4 + quarter):
                    e.dma_start(
                        out=bass.AP(t_a2i, (jj * NH * 64 + h * 64) * TQ + coff,
                                    [[TQ, 64], [1, QC]]),
                        in_=aos[cpar][:, :]).then_inc(s["aodma"].h, 16)
            s["aodma"].inc(32)

    def emit_pend_fin():
        pf = chst["pend_fin"]
        if pf is None:
            return
        chst["pend_fin"] = None
        h, qc, cpar = pf

        @P.on("vector")
        def _(e, w=s["pv"].n, wr=s["rcp"].n):
            e.wait_ge(s["pv"].h, w)
            if wr > 0:
                e.wait_ge(s["rcp"].h, wr)
            e.tensor_copy(dnm[0:1, :], ps_pv[64:65, :]).then_inc(s["dnm"].h, 1)
        s["dnm"].inc()

        @P.on("scalar")
        def _(e, w=s["dnm"].n, wbc=s["bc"].n):
            e.wait_ge(s["dnm"].h, w)
            if wbc > 0:
                e.wait_ge(s["bc"].h, wbc)
            e.activation(rcpr[0:1, 0:QC], dnm[0:1, :], AF.Ln)
            e.activation(rcpr[0:1, QC:2 * QC], rcpr[0:1, 0:QC], AF.Exp,
                         scale=-1.0).then_inc(s["rcp"].h, 1)
        s["rcp"].inc()

        chst["pend_tail"] = [("bc", h, qc, cpar, half * 512) for half in range(NB)]
        chst["pend_tail"].append(("dma", h, qc, cpar, 0))

    def chunk_gen(c):
        emit_pend_fin()
        qc, h = c // NH, c % NH
        cpar = c % 2
        nkb = (qc + 1) * KPC
        dstart = qc * KPC
        order = list(range(dstart, nkb)) + list(range(0, dstart))
        first_b, last_b = {}, {}
        for ki, kb in enumerate(order):
            diag = kb >= dstart
            b0 = ((kb - dstart) * 128) // 512 if diag else 0
            for b in range(b0, NB):
                if b not in first_b:
                    first_b[b] = ki
                last_b[b] = ki
        pend_pv = []

        def emit_pv(spec):
            ki, kb, off, nblk, dep = spec
            kpar4 = ki % 3

            @P.on("tensor")
            def _(e, dep=dep, ki=ki, kb=kb, off=off, nblk=nblk, kpar4=kpar4,
                  wao=s["ao"].n if ki == 0 else 0,
                  wvv=pref_vev[qc] if ki == 0 else 0, wm=MSC_ALL):
                e.wait_ge(s[dep[0]].h, dep[1])
                if ki == 0:
                    if wao > 0:
                        e.wait_ge(s["ao"].h, wao)
                    e.wait_ge(s["vev"].h, wvv)
                    e.wait_ge(s["msc"].h, wm)
                last = None
                for b in range(nblk):
                    o = off + b * 512
                    bi = o // 512
                    last = e.matmul(ps_pv[:, o:o + 512],
                                    Vx[:, kb * NH * 65 + h * 65: kb * NH * 65 + h * 65 + 65],
                                    pt[kpar4][:, o:o + 512],
                                    start=(first_b[bi] == ki), stop=(last_b[bi] == ki))
                last.then_inc(s["pv"].h, 1)
            s["pv"].inc()
            chst["pt_free"][kpar4] = s["pv"].n

        for ki, kb in enumerate(order):
            kpar = ki % 2
            kpar4 = ki % 3
            diag = kb >= dstart
            c0 = (kb - dstart) * 128 if diag else 0
            b0 = (c0 // 512) if diag else 0
            off = 512 * b0
            nblk = NB - b0

            @P.on("tensor")
            def _(e, h=h, qc=qc, kb=kb, ki=ki, kpar=kpar, off=off, nblk=nblk,
                  wexp=chst["sc_free"][kpar],
                  wqv=pref_qev[qc] if ki == 0 else 0,
                  wkv=pref_kev[qc] if ki == 0 else 0):
                if ki == 0:
                    e.wait_ge(s["qev"].h, wqv)
                    e.wait_ge(s["kev"].h, wkv)
                if wexp > 0:
                    e.wait_ge(s["exp"].h, wexp)
                last = None
                for b in range(nblk):
                    o = off + b * 512
                    last = e.matmul(ps_sc[kpar][:, o:o + 512],
                                    Kfm[:, h * S + kb * 128: h * S + (kb + 1) * 128],
                                    qall[qc % 2][:, h * QC + o: h * QC + o + 512],
                                    start=True, stop=True)
                last.then_inc(s["sc"].h, 1)
            s["sc"].inc()

            if chst["pend_tail"] and 1 <= ki <= len(chst["pend_tail"]):
                emit_tail_piece(chst["pend_tail"][ki - 1])
                if ki == len(chst["pend_tail"]):
                    chst["pend_tail"] = []

            @P.on("scalar")
            def _(e, h=h, qc=qc, ki=ki, kpar=kpar, kpar4=kpar4, off=off,
                  w=s["sc"].n, wpv=chst["pt_free"][kpar4],
                  wnf=pref_nfz[qc] if ki == 0 else 0):
                if ki == 0:
                    e.wait_ge(s["nfz"].h, wnf)
                e.wait_ge(s["sc"].h, w)
                if wpv > 0:
                    e.wait_ge(s["pv"].h, wpv)
                e.activation(pt[kpar4][:, off:QC], ps_sc[kpar][:, off:QC], AF.Exp,
                             bias=negM[:, (qc % 2) * NH + h:(qc % 2) * NH + h + 1],
                             scale=0.125).then_inc(s["exp"].h, 1)
            s["exp"].inc()
            chst["sc_free"][kpar] = s["exp"].n

            dep = ("exp", s["exp"].n)
            if diag:
                L = c0 + 128 - off

                @P.on("vector")
                def _(e, kpar4=kpar4, off=off, L=L, w=s["exp"].n):
                    e.wait_ge(s["exp"].h, w)
                    e.tensor_tensor(ptm[:, 0:L], pt[kpar4][:, off:off + L],
                                    mkz[:, 512 - L:512], ALU.mult)
                    e.tensor_copy(pt[kpar4][:, off:off + L], ptm[:, 0:L]).then_inc(s["msk"].h, 1)
                s["msk"].inc()
                dep = ("msk", s["msk"].n)

            pend_pv.append((ki, kb, off, nblk, dep))
            if len(pend_pv) > PVLAG:
                emit_pv(pend_pv.pop(0))
            yield

        for piece in chst["pend_tail"]:
            emit_tail_piece(piece)
        chst["pend_tail"] = []
        for spec in pend_pv:
            emit_pv(spec)

        chst["pend_fin"] = (h, qc, cpar)
        pref_exp[qc] = s["exp"].n
        if h == NH - 1:
            pref_sc2[qc] = s["sc"].n

    ready = []
    sched = dict(active=None, done=0)
    GAPCAP = 3 * KPC
    GAP_IN0 = 0
    GAP_IN1 = 0

    def emit_quantum(cap):
        n = 0
        while n < cap and (sched["active"] is not None or ready):
            if sched["active"] is None:
                sched["active"] = chunk_gen(ready.pop(0))
            try:
                next(sched["active"])
                n += 1
            except StopIteration:
                sched["active"] = None
                sched["done"] += 1

    def force_chunks(n_done):
        while sched["done"] < n_done and (sched["active"] is not None or ready):
            emit_quantum(1)

    tgs = [tile_gen(i) for i in range(NT)]
    next(tgs[0])
    for i in range(NT):
        if i % KPC == 0:
            force_chunks((i // KPC - 1) * NH)
        next(tgs[i])
        if i + 1 < NT:
            next(tgs[i + 1])
        emit_quantum(GAP_IN0)
        next(tgs[i])
        emit_quantum(GAP_IN1)
        for _ in tgs[i]:
            pass
        if (i + 1) % KPC == 0:
            q = (i + 1) // KPC - 1
            force_chunks((q - 1) * NH)
            emit_finalize(q)
            for h in range(NH):
                ready.append(q * NH + h)
        emit_quantum(GAPCAP)
    emit_quantum(10 ** 9)
    emit_pend_fin()
    for piece in chst["pend_tail"]:
        emit_tail_piece(piece)
    chst["pend_tail"] = []

    if dbg:
        @P.on("gpsimd")
        def _(e, w=s["nfz"].n, w2=s["kev"].n, w3=s["vev"].n):
            e.wait_ge(s["nfz"].h, w)
            e.wait_ge(s["kev"].h, w2)
            e.wait_ge(s["vev"].h, w3)
            e.dma_start(out=d_okfm[:, :], in_=Kfm[:, :]).then_inc(s["outd"].h, 16)
            e.dma_start(out=d_ovx[:, :], in_=Vx[:, :]).then_inc(s["outd"].h, 16)
            e.dma_start(out=d_onegm[:, :], in_=negM[:, :]).then_inc(s["outd"].h, 16)
        s["outd"].inc(48)

    p3_done = dict(aodma=s["aodma"].n)
    P.flush()
    es0.close()
    esM.close()

    es4 = ExitStack()
    E4 = es4.enter_context
    wfst = [E4(nc.sbuf_tensor(f"wfst{j}", [128, 1024], F32)) for j in range(2)]
    wo_bf = E4(nc.sbuf_tensor("wo_bf", [128, DC * 1024], BF16))
    wost = [E4(nc.sbuf_tensor(f"wost{j}", [128, 1024], F32)) for j in range(2)]
    dsh2 = E4(nc.sbuf_tensor("dsh2", [128, 1024], F32))
    weff = E4(nc.sbuf_tensor("weff", [128, DC * 1024], BF16))
    ao = E4(nc.sbuf_tensor("ao", [128, DC * TQ], BF16))
    x1 = E4(nc.sbuf_tensor("x1", [128, NTQ * D], F32))
    xrs = [E4(nc.sbuf_tensor(f"xrs{j}", [128, D], F32)) for j in range(2)]
    ysq2 = E4(nc.sbuf_tensor("ysq2", [128, D], F32))
    ysq3 = E4(nc.sbuf_tensor("ysq3", [128, D], F32))
    ysq4 = E4(nc.sbuf_tensor("ysq4", [128, D], F32))
    lsm2 = E4(nc.sbuf_tensor("lsm2", [128, 16], F32))
    lsm3 = E4(nc.sbuf_tensor("lsm3", [128, 16], F32))
    lsm4 = E4(nc.sbuf_tensor("lsm4", [128, 16], F32))
    xh2 = [E4(nc.sbuf_tensor(f"xh2{j}", [128, D], BF16)) for j in range(2)]
    xh2T = [E4(nc.sbuf_tensor(f"xh2T{j}", [128, D], BF16)) for j in range(2)]
    outt = [E4(nc.sbuf_tensor(f"outt{j}", [128, D], F32)) for j in range(2)]
    ps_y = [E4(nc.psum_tensor(f"ps_y{j}", [128, 1024], F32)) for j in range(2)]
    ps_t2 = [E4(nc.psum_tensor(f"ps_t2{j}", [128, D], BF16)) for j in range(2)]

    @P.on("gpsimd")
    def _(e, w=p3_done["aodma"]):
        e.wait_ge(s["aodma"].h, w)
        with allow_ici_all_to_all():
            e.collective_compute(
                "AllToAll", ALU.bypass,
                replica_groups=[[0, 1, 2, 3, 4, 5, 6, 7]],
                ins=[t_a2i.ap().opt()],
                outs=[t_a2o.ap().opt()],
            ).then_inc(s["cc"].h, 1)
    s["cc"].inc()

    @P.on("sync")
    def _(e, w=p3_done["aodma"]):
        e.wait_ge(s["aodma"].h, w)
        e.dma_start(out=dsh2[0:64, :], in_=d_dsh[:, :]).then_inc(s["dshld"].h, 16)
        e.dma_start(out=dsh2[64:128, :], in_=d_dsh[:, :]).then_inc(s["dshld"].h, 16)
    s["dshld"].inc(32)

    for jj in range(DC):
        @P.on("sync")
        def _(e, jj=jj, smf="wfld" if jj % 2 == 0 else "wfldB", wfree=s["wft"].n - 1):
            if wfree > 0:
                e.wait_ge(s["wft"].h, wfree)
            e.dma_start(out=wfst[jj % 2][:, :],
                        in_=d_wf[jj * 128:(jj + 1) * 128, :]).then_inc(s[smf].h, 16)
        smf = "wfld" if jj % 2 == 0 else "wfldB"
        s[smf].inc(16)

        @P.on("tensor")
        def _(e, jj=jj, smf=smf, w=s[smf].n, wv=s["weff"].n - 1):
            e.wait_ge(s[smf].h, w)
            if jj >= 2:
                e.wait_ge(s["weff"].h, wv)
            last = None
            for c in range(DC):
                last = e.transpose(ps_y[jj % 2][:, c * 128:(c + 1) * 128],
                                   wfst[jj % 2][:, c * 128:(c + 1) * 128],
                                   idf[:, :])
            last.then_inc(s["wft"].h, 1)
        s["wft"].inc()

        @P.on("vector")
        def _(e, jj=jj, w=s["wft"].n, wd=s["dshld"].n):
            e.wait_ge(s["wft"].h, w)
            e.wait_ge(s["dshld"].h, wd)
            for c in range(DC):
                ov = weff[:, c * 1024 + jj * 128: c * 1024 + (jj + 1) * 128]
                e.tensor_tensor(ov, ps_y[jj % 2][:, c * 128:(c + 1) * 128],
                                dsh2[:, jj * 128:(jj + 1) * 128], ALU.add)
            e.nop().then_inc(s["weff"].h, 1)
        s["weff"].inc()

    for c in range(DC):
        stg = wost[c % 2]

        smo = "wo_ld" if c % 2 == 0 else "wo_ldB"

        @P.on("sync")
        def _(e, c=c, stg=stg, smo=smo, wfree=s["wo_c"].n - 1):
            if wfree > 0:
                e.wait_ge(s["wo_c"].h, wfree)
            e.dma_start(out=stg[:, :], in_=d_wo[c * 128:(c + 1) * 128, :]).then_inc(s[smo].h, 16)
        s[smo].inc(16)

        @P.on("vector")
        def _(e, c=c, stg=stg, smo=smo, w=s[smo].n):
            e.wait_ge(s[smo].h, w)
            e.tensor_copy(wo_bf[:, c * 1024:(c + 1) * 1024], stg[:, :]).then_inc(s["wo_c"].h, 1)
        s["wo_c"].inc()

    @P.on("gpsimd")
    def _(e, w=s["cc"].n, wv=s["weff"].n, wl=16 * 6):
        e.wait_ge(s["wld"].h, wl)
        e.wait_ge(s["cc"].h, w)
        e.wait_ge(s["weff"].h, wv)
        e.wait_ge(s["wo_c"].h, DC)
        with e.register("gb0") as r0, e.register("gb1") as r1:
            e.reg_load(r0, gbt[0:1, 0:1])
            for c in range(DC):
                e.reg_mov(r1, c * 128 * TQ)
                e.reg_add(r1, r1, r0)
                e.dma_start(out=ao[:, c * TQ:(c + 1) * TQ],
                            in_=bass.AP(t_a2o, r1, [[TQ, 128], [1, TQ]])).then_inc(s["aold"].h, 16)
    s["aold"].inc(16 * DC)

    for i in range(NTQ):
        j = i % 2

        smr = "xr" if j == 0 else "xrB"

        @P.on("sync")
        def _(e, i=i, j=j, smr=smr, wfree=s["x1"].n - 1):
            if i >= 2:
                e.wait_ge(s["x1"].h, wfree)
            e.dma_start(out=xrs[j][:, :], in_=d_xres[i * 128:(i + 1) * 128, :]).then_inc(s[smr].h, 16)
        s[smr].inc(16)

        @P.on("tensor")
        def _(e, i=i, j=j, w=s["aold"].n, wfree=s["x1"].n - 1):
            e.wait_ge(s["aold"].h, w)
            if i >= 2:
                e.wait_ge(s["x1"].h, wfree)
            last = None
            for c in range(DC):
                for bl in range(2):
                    last = e.matmul(ps_y[j][:, bl * 512:(bl + 1) * 512],
                                    ao[:, c * TQ + i * 128: c * TQ + (i + 1) * 128],
                                    wo_bf[:, c * 1024 + bl * 512: c * 1024 + (bl + 1) * 512],
                                    start=(c == 0), stop=(c == DC - 1))
            last.then_inc(s["ymm"].h, 1)
        s["ymm"].inc()

        sqw = [("ymm", s["ymm"].n)] + ([("x1", s["x1"].n)] if i >= 1 else [])
        emit_ln(P, s, "sqx", ps_y[j][:, :], ysq2[:, :], lsm2, epsT, sqw, [],
                chain_waits=([("x1", s["x1"].n)] if i >= 1 else []))

        @P.on("vector")
        def _(e, i=i, j=j, smr=smr, w=s["sqx"].n, wx=s[smr].n):
            e.wait_ge(s["sqx"].h, w)
            e.wait_ge(s[smr].h, wx)
            e.tensor_scalar(ysq2[:, :], ps_y[j][:, :], lsm2[:, 9:10], lsm2[:, 8:9],
                            ALU.add, ALU.mult)
            e.tensor_tensor(x1[:, i * D:(i + 1) * D], xrs[j][:, :], ysq2[:, :],
                            ALU.add).then_inc(s["x1"].h, 1)
        s["x1"].inc()

    p4_done = s["x1"].n
    for i in range(NTQ):
        j = i % 2

        sqw = [("x1", i + 1)] + ([("xh2", s["xh2"].n - 2)] if i >= 1 else [])
        emit_ln(P, s, "xh2", x1[:, i * D:(i + 1) * D], ysq3[:, :], lsm3, epsT, sqw, [])

        @P.on("vector")
        def _(e, i=i, j=j, w=s["xh2"].n, wt=s["tp2"].n - 1):
            e.wait_ge(s["xh2"].h, w)
            if i >= 2:
                e.wait_ge(s["tp2"].h, wt)
            e.tensor_scalar(xh2[j][:, :], x1[:, i * D:(i + 1) * D], lsm3[:, 9:10],
                            lsm3[:, 8:9], ALU.add, ALU.mult).then_inc(s["xh2"].h, 1)
        s["xh2"].inc()

        @P.on("tensor")
        def _(e, i=i, j=j, w=s["xh2"].n, wfree=s["xh2T"].n - 1):
            e.wait_ge(s["xh2"].h, w)
            if i >= 2:
                e.wait_ge(s["xh2T"].h, wfree)
            last = None
            for c in range(DC):
                last = e.transpose(ps_t2[j][:, c * 128:(c + 1) * 128],
                                   xh2[j][:, c * 128:(c + 1) * 128], idb[:, :])
            last.then_inc(s["tp2"].h, 1)
        s["tp2"].inc()

        @P.on("scalar")
        def _(e, j=j, w=s["tp2"].n):
            e.wait_ge(s["tp2"].h, w)
            e.copy(xh2T[j][:, :], ps_t2[j][:, :]).then_inc(s["xh2T"].h, 1)
        s["xh2T"].inc()

        @P.on("tensor")
        def _(e, i=i, j=j, w=s["xh2T"].n, wfree=s["outp"].n - 1):
            e.wait_ge(s["xh2T"].h, w)
            if i < 2:
                e.wait_ge(s["x1"].h, p4_done)
            if i >= 2:
                e.wait_ge(s["outp"].h, wfree)
            last = None
            for c in range(DC):
                for bl in range(2):
                    last = e.matmul(ps_y[j][:, bl * 512:(bl + 1) * 512],
                                    xh2T[j][:, c * 128:(c + 1) * 128],
                                    weff[:, c * 1024 + bl * 512: c * 1024 + (bl + 1) * 512],
                                    start=(c == 0), stop=(c == DC - 1))
            last.then_inc(s["y2"].h, 1)
        s["y2"].inc()

        sqw = [("y2", s["y2"].n)] + ([("outp", s["outp"].n)] if i >= 1 else [])
        emit_ln(P, s, "y2", ps_y[j][:, :], ysq4[:, :], lsm4, epsT, sqw, [],
                chain_waits=([("outp", s["outp"].n)] if i >= 1 else []))

        smd = "outd" if j == 0 else "outdB"

        @P.on("vector")
        def _(e, i=i, j=j, smd=smd, w=s["y2"].n, wd=s["outd" if j == 0 else "outdB"].n):
            e.wait_ge(s["y2"].h, w)
            if i >= 2:
                e.wait_ge(s[smd].h, wd)
            e.tensor_scalar(ysq4[:, :], ps_y[j][:, :], lsm4[:, 9:10], lsm4[:, 8:9],
                            ALU.add, ALU.mult)
            e.tensor_tensor(outt[j][:, :], x1[:, i * D:(i + 1) * D], ysq4[:, :],
                            ALU.add).then_inc(s["outp"].h, 1)
        s["outp"].inc()

        @P.on("gpsimd")
        def _(e, i=i, j=j, smd=smd, w=s["outp"].n):
            e.wait_ge(s["outp"].h, w)
            e.dma_start(out=d_out[i * 128:(i + 1) * 128, :], in_=outt[j][:, :]).then_inc(s[smd].h, 16)
        s[smd].inc(16)

    if dbg:
        @P.on("gpsimd")
        def _(e, wa=s["aold"].n, wx=s["x1"].n, ww=s["weff"].n):
            e.wait_ge(s["aold"].h, wa)
            e.wait_ge(s["x1"].h, wx)
            e.wait_ge(s["weff"].h, ww)
            e.dma_start(out=d_oao[:, :], in_=ao[:, :]).then_inc(s["outd"].h, 16)
            e.dma_start(out=d_owob[:, :], in_=wo_bf[:, :]).then_inc(s["outd"].h, 16)
            e.dma_start(out=d_ox1[:, :], in_=x1[:, :]).then_inc(s["outd"].h, 16)
            e.dma_start(out=d_oweff[:, :], in_=weff[:, :]).then_inc(s["outd"].h, 16)
        s["outd"].inc(64)

    @P.on("gpsimd")
    def _(e, w=s["outd"].n, wb=s["outdB"].n):
        e.wait_ge(s["outd"].h, w)
        if wb > 0:
            e.wait_ge(s["outdB"].h, wb)

    P.flush()
    block_cm.__exit__(None, None, None)
    es4.close()
    es_all.close()
    return nc


def _make_aux(cfg):
    S, NT = cfg["S"], cfg["NT"]
    g = np.asarray(RIEMANN, dtype=np.float32)[:32]
    inv = (np.float32(1.0) / (g / g[0])).astype(np.float32)
    pos = np.arange(S, dtype=np.float32)
    th = pos[:, None] * inv[None, :]
    th0 = np.ascontiguousarray(th.reshape(NT, 128, 32).transpose(1, 0, 2).reshape(128, NT * 32))
    idb = np.eye(128, dtype=ml_dtypes.bfloat16)
    idf = np.eye(128, dtype=np.float32)
    mkz = np.zeros((128, 512), dtype=np.float32)
    mkz[:, 384:512] = (np.arange(128)[None, :] >= np.arange(128)[:, None])
    mkz = mkz.astype(ml_dtypes.bfloat16)
    one = np.ones((1, 128), dtype=np.float32)
    return th0, idb, idf, mkz, one


def kernel(L0_attn_x, L0_attn_w_qkv, L0_attn_w_o, L0_attn_log_scale,
           L0_ffn_weight, L0_ffn_disha, _cfg=None, _ret_nc=False, _dbg=False):
    cfg = _cfg or make_cfg()
    S, TQ = cfg["S"], cfg["TQ"]
    th0, idb, idf, mkz, one = _make_aux(cfg)
    x = np.asarray(L0_attn_x, dtype=np.float32)
    wqkv_f = np.asarray(L0_attn_w_qkv, dtype=np.float32)
    wo = np.ascontiguousarray(np.asarray(L0_attn_w_o, dtype=np.float32))
    lsc = np.asarray(L0_attn_log_scale, dtype=np.float32)
    wf = np.ascontiguousarray(np.asarray(L0_ffn_weight, dtype=np.float32))
    dsh = np.ascontiguousarray(np.asarray(L0_ffn_disha, dtype=np.float32))

    in_maps = []
    for c in range(8):
        B, R = c // 4, c % 4
        H0 = 4 * R
        in_maps.append({
            "x": np.ascontiguousarray(x[B]),
            "xres": np.ascontiguousarray(x[B, R * TQ:(R + 1) * TQ]),
            "wq": np.ascontiguousarray(wqkv_f[:, H0 * 64:H0 * 64 + 256]),
            "wk": np.ascontiguousarray(wqkv_f[:, 1024 + H0 * 64:1024 + H0 * 64 + 256]),
            "wv": np.ascontiguousarray(wqkv_f[:, 2048 + H0 * 64:2048 + H0 * 64 + 256]),
            "wo": wo, "wf": wf, "dsh": dsh,
            "lsc": np.ascontiguousarray(lsc[H0:H0 + 4].reshape(1, 4)),
            "th0": th0, "idb": idb, "idf": idf, "mkz": mkz, "one": one,
            "gb": np.array([[B * 4 * 256 * TQ]], dtype=np.int32),
        })
    nc = build_program(cfg, dbg=_dbg)
    if _ret_nc:
        return nc, in_maps
    res = run_bass_kernel_spmd(nc, in_maps, core_ids=list(range(8)))
    out = np.empty((2, S, 1024), dtype=np.float32)
    for c in range(8):
        B, R = c // 4, c % 4
        out[B, R * TQ:(R + 1) * TQ] = res.results[c]["out"]
    if _dbg:
        return out, res.results
    return out


# version 09
# speedup vs baseline: 1.0119x; 1.0085x over previous
from contextlib import ExitStack

import numpy as np
import ml_dtypes

import concourse.bass as bass
import concourse.mybir as mybir
from concourse.bass_utils import run_bass_kernel_spmd
from concourse.aot_env import allow_ici_all_to_all

F32 = mybir.dt.float32
BF16 = mybir.dt.bfloat16
I32 = mybir.dt.int32
AF = mybir.ActivationFunctionType
ALU = mybir.AluOpType
AX = mybir.AxisListType.X

PI = float(np.pi)
PI_IN = float(np.nextafter(np.float32(PI), np.float32(0.0)))
INV2PI = float(np.float32(1.0 / (2 * np.pi)))
C1 = 6.28125
_r1 = 2 * np.pi - C1
C2 = float(np.float32(np.ldexp(np.round(np.ldexp(_r1, 19)), -19)))
C3 = float(np.float32(_r1 - C2))
C23 = float(np.float32(C2 + C3))
EPS = 1e-5

RIEMANN = (14.134725142, 21.022039639, 25.01085758, 30.424876126, 32.935061588,
           37.586178159, 40.918719012, 43.327073281, 48.005150881, 49.773832478,
           52.970321478, 56.446247697, 59.347044003, 60.831778525, 65.112544048,
           67.079810529, 69.546401711, 72.067157674, 75.704690699, 77.144840069,
           79.33737502, 82.910380854, 84.735492981, 87.425274613, 88.809111208,
           92.491899271, 94.651344041, 95.870634228, 98.831194218, 101.317851006,
           103.72553804, 105.446623052)


def make_cfg(S=8192, QC=1024):
    return dict(S=S, D=1024, QC=QC, NT=S // 128, NQC=S // QC, NB=QC // 512,
                NH=4, DH=64, DC=8, TQ=S // 4, NTQ=S // 4 // 128)


class Prog:

    ENGS = ("sync", "gpsimd", "vector", "scalar", "tensor")

    def __init__(self, block):
        self.block = block
        self.buf = {k: [] for k in self.ENGS}

    def on(self, eng):
        def deco(fn):
            self.buf[eng].append(fn)
            return None
        return deco

    def flush(self):
        for eng in self.ENGS:
            fns = self.buf[eng]
            if not fns:
                continue

            def section(e, fns=list(fns)):
                for f in fns:
                    f(e)
            getattr(self.block, eng)(section)
            self.buf[eng] = []


class Sem:
    def __init__(self, nc, es, name):
        self.h = es.enter_context(nc.semaphore(name))
        self.n = 0

    def inc(self, k=1):
        self.n += k
        return self.n



def emit_ln(P, s, sem_name, x_ap, xsq, lsm, epsT, sq_waits, red_waits, chain_waits=()):
    import concourse.mybir as _mb
    D = 1024

    @P.on("scalar")
    def _(e, waits=tuple(sq_waits)):
        for (ss, v) in waits:
            e.wait_ge(s[ss].h, v)
        e.activation(xsq, x_ap, AF.Square).then_inc(s[sem_name].h, 1)
    s[sem_name].inc()

    @P.on("vector")
    def _(e, w=s[sem_name].n, waits=tuple(red_waits)):
        for (ss, v) in waits:
            e.wait_ge(s[ss].h, v)
        e.wait_ge(s[sem_name].h, w)
        e.reduce_sum(lsm[:, 0:1], x_ap, axis=AX)
        e.reduce_sum(lsm[:, 1:2], xsq, axis=AX).then_inc(s[sem_name].h, 1)
    s[sem_name].inc()

    @P.on("scalar")
    def _(e, w=s[sem_name].n, waits=tuple(chain_waits)):
        for (ss, v) in waits:
            e.wait_ge(s[ss].h, v)
        e.wait_ge(s[sem_name].h, w)
        e.activation(lsm[:, 3:4], lsm[:, 0:1], AF.Square, scale=1.0 / D)
        e.activation(lsm[:, 4:5], lsm[:, 3:4], AF.Identity, scale=-1.0)
        e.activation(lsm[:, 2:3], lsm[:, 0:1], AF.Identity, scale=1.0 / D)
        e.activation(lsm[:, 9:10], lsm[:, 2:3], AF.Identity, scale=-1.0)
        e.activation(lsm[:, 5:6], lsm[:, 1:2], AF.Identity, scale=1.0 / D,
                     bias=lsm[:, 4:5])
        e.activation(lsm[:, 6:7], lsm[:, 5:6], AF.Identity, bias=epsT[:, 0:1])
        e.activation(lsm[:, 7:8], lsm[:, 6:7], AF.Ln)
        e.activation(lsm[:, 8:9], lsm[:, 7:8], AF.Exp, scale=-0.5).then_inc(s[sem_name].h, 1)
    s[sem_name].inc()


def build_program(cfg, dbg=False):
    S, D, QC = cfg["S"], cfg["D"], cfg["QC"]
    NT, NQC, NB = cfg["NT"], cfg["NQC"], cfg["NB"]
    NH, DH, DC = cfg["NH"], cfg["DH"], cfg["DC"]
    TQ, NTQ = cfg["TQ"], cfg["NTQ"]
    KPC = QC // 128

    nc = bass.Bass("TRN2", target_bir_lowering=False)

    d_x = nc.dram_tensor("x", [S, D], F32, kind="ExternalInput").ap()
    d_xres = nc.dram_tensor("xres", [TQ, D], F32, kind="ExternalInput").ap()
    d_wq = nc.dram_tensor("wq", [D, NH * DH], F32, kind="ExternalInput").ap()
    d_wk = nc.dram_tensor("wk", [D, NH * DH], F32, kind="ExternalInput").ap()
    d_wv = nc.dram_tensor("wv", [D, NH * DH], F32, kind="ExternalInput").ap()
    d_wo = nc.dram_tensor("wo", [D, D], F32, kind="ExternalInput").ap()
    d_wf = nc.dram_tensor("wf", [D, D], F32, kind="ExternalInput").ap()
    d_dsh = nc.dram_tensor("dsh", [64, D], F32, kind="ExternalInput").ap()
    d_lsc = nc.dram_tensor("lsc", [1, NH], F32, kind="ExternalInput").ap()
    d_th0 = nc.dram_tensor("th0", [128, NT * 32], F32, kind="ExternalInput").ap()
    d_idb = nc.dram_tensor("idb", [128, 128], BF16, kind="ExternalInput").ap()
    d_idf = nc.dram_tensor("idf", [128, 128], F32, kind="ExternalInput").ap()
    d_mkz = nc.dram_tensor("mkz", [128, 512], BF16, kind="ExternalInput").ap()
    d_one = nc.dram_tensor("one", [1, 128], F32, kind="ExternalInput").ap()
    d_gb = nc.dram_tensor("gb", [1, 1], I32, kind="ExternalInput").ap()
    d_out = nc.dram_tensor("out", [TQ, D], F32, kind="ExternalOutput").ap()

    t_a2i = nc.dram_tensor("a2i", [8 * NH * 64, TQ], BF16)
    t_a2o = nc.dram_tensor("a2o", [8 * NH * 64, TQ], BF16)
    if dbg:
        d_okfm = nc.dram_tensor("o_kfm", [64, NH * S], BF16, kind="ExternalOutput").ap()
        d_ovx = nc.dram_tensor("o_vx", [128, NT * NH * 65], BF16, kind="ExternalOutput").ap()
        d_onegm = nc.dram_tensor("o_negm", [128, 2 * NH], F32, kind="ExternalOutput").ap()

    es_all = ExitStack()
    E = es_all.enter_context

    s = {}
    for nm in ["x", "th", "ln", "sqx", "tpx", "xhT", "qkv", "trig", "sin",
               "rope", "sq", "nrm", "ktr", "kev", "bar", "wld", "wcast",
               "qtr", "qev", "sc", "exp", "msk", "pv", "dnm", "rcp", "bc",
               "rb", "ao", "aodma", "cc", "aold", "wfld", "wft", "weff",
               "xr", "ymm", "x1", "xh2", "tp2", "xh2T", "y2", "outp",
               "outd", "wo_ld", "wo_c", "dshld", "msc",
               "xB", "thB", "wkld", "wkldB", "wo_ldB", "wfldB", "xrB", "outdB",
               "vmm", "vev", "nfz"]:
        s[nm] = Sem(nc, es_all, "s_" + nm)

    scbc = E(nc.sbuf_tensor("scbc", [128, NH], F32))
    negM = E(nc.sbuf_tensor("negM", [128, 2 * NH], F32))
    nrm = [E(nc.sbuf_tensor(f"nrm{j}", [128, 8], F32)) for j in range(2)]
    epsT = E(nc.sbuf_tensor("epsT", [128, 1], F32))
    idb = E(nc.sbuf_tensor("sb_idb", [128, 128], BF16))
    idf = E(nc.sbuf_tensor("sb_idf", [128, 128], F32))
    mkz = E(nc.sbuf_tensor("sb_mkz", [128, 512], BF16))
    onef = E(nc.sbuf_tensor("onef", [1, 128], F32))
    lscT = E(nc.sbuf_tensor("lscT", [1, NH], F32))
    gbt = E(nc.sbuf_tensor("gbt", [1, 1], I32))

    esM = ExitStack()
    EM = esM.enter_context
    Kfm = EM(nc.sbuf_tensor("Kfm", [64, NH * S], BF16))
    Vx = EM(nc.sbuf_tensor("Vx", [128, NT * NH * 65], BF16))
    qall = [EM(nc.sbuf_tensor(f"qall{j}", [64, NH * QC], BF16)) for j in range(2)]
    wqkv = EM(nc.sbuf_tensor("wqkv", [128, DC * 768], BF16))
    xstg = [EM(nc.sbuf_tensor(f"xstg{j}", [128, D], F32)) for j in range(2)]
    xsq = EM(nc.sbuf_tensor("xsq", [128, D], F32))
    xhat = [EM(nc.sbuf_tensor(f"xhat{j}", [128, D], BF16)) for j in range(2)]
    xhT = [EM(nc.sbuf_tensor(f"xhT{j}", [128, D], BF16)) for j in range(2)]
    Qtm = [EM(nc.sbuf_tensor(f"Qtm{j}", [128, 256], BF16)) for j in range(2)]
    krot = [EM(nc.sbuf_tensor(f"krot{j}", [128, 256], BF16)) for j in range(2)]
    TB = 8
    NBATCH = (NT + TB - 1) // TB
    th0b = [EM(nc.sbuf_tensor(f"th0b{j}", [128, TB * 32], F32)) for j in range(2)]
    cosr = [EM(nc.sbuf_tensor(f"cosr{j}", [128, TB * 128], BF16)) for j in range(2)]
    sinr = [EM(nc.sbuf_tensor(f"sinr{j}", [128, TB * 128], BF16)) for j in range(2)]
    thv = EM(nc.sbuf_tensor("thv", [128, 256], F32))
    tki = EM(nc.sbuf_tensor("tki", [128, 256], I32))
    tkf = EM(nc.sbuf_tensor("tkf", [128, 256], F32))
    tg1 = EM(nc.sbuf_tensor("tg1", [128, 256], F32))
    tg2 = EM(nc.sbuf_tensor("tg2", [128, 256], F32))
    tg3 = EM(nc.sbuf_tensor("tg3", [128, 256], F32))
    tu1 = EM(nc.sbuf_tensor("tu1", [128, 256], F32))
    tu2 = EM(nc.sbuf_tensor("tu2", [128, 256], F32))
    trs = [EM(nc.sbuf_tensor(f"trs{j}", [128, 256], F32)) for j in range(2)]
    trc = [EM(nc.sbuf_tensor(f"trc{j}", [128, 256], F32)) for j in range(2)]
    tmpa = EM(nc.sbuf_tensor("tmpa", [128, 128], F32))
    tmpb = EM(nc.sbuf_tensor("tmpb", [128, 128], F32))
    sqs = [EM(nc.sbuf_tensor(f"sqs{j}", [128, 512], F32)) for j in range(2)]
    lsm = EM(nc.sbuf_tensor("lsm", [128, 16], F32))
    nrw = [EM(nc.sbuf_tensor(f"nrw{j}", [128, 8], F32)) for j in range(2)]
    nfin = EM(nc.sbuf_tensor("nfin", [128, 8], F32))
    nfs = EM(nc.sbuf_tensor("nfs", [128, 512], F32))
    pt = [EM(nc.sbuf_tensor(f"pt{j}", [128, QC], BF16)) for j in range(3)]
    aos = [EM(nc.sbuf_tensor(f"aos{j}", [64, QC], BF16)) for j in range(2)]
    dnm = EM(nc.sbuf_tensor("dnm", [1, QC], F32))
    rcpr = EM(nc.sbuf_tensor("rcpr", [1, 2 * QC], F32))
    rbt = EM(nc.sbuf_tensor("rbt", [64, 512], F32))
    ptm = EM(nc.sbuf_tensor("ptm", [128, 512], BF16))
    ps_sc = [EM(nc.psum_tensor(f"ps_sc{j}", [128, QC], F32)) for j in range(2)]
    ps_pv = EM(nc.psum_tensor("ps_pv", [65, QC], F32))
    ps_f32 = EM(nc.psum_tensor("ps_f32", [128, 512], F32))
    ps_bf = EM(nc.psum_tensor("ps_bf", [128, D], BF16))

    block_cm = nc.Block()
    block = block_cm.__enter__()
    P = Prog(block)

    es0 = ExitStack()
    E0 = es0.enter_context
    wstg = [E0(nc.sbuf_tensor("wstg0", [128, 768], F32))]

    @P.on("sync")
    def _(e):
        for dst, src in ((idb, d_idb), (idf, d_idf), (mkz, d_mkz),
                         (onef, d_one), (lscT, d_lsc), (gbt, d_gb)):
            e.dma_start(out=dst[:, :], in_=src[:, :]).then_inc(s["wld"].h, 16)
    s["wld"].inc(96)

    @P.on("scalar")
    def _(e, w=s["wld"].n):
        e.wait_ge(s["wld"].h, w)
        e.activation(lscT[:, :], lscT[:, :], AF.Exp).then_inc(s["msc"].h, 1)
    s["msc"].inc()

    @P.on("tensor")
    def _(e, w=s["msc"].n):
        e.wait_ge(s["msc"].h, w)
        e.matmul(ps_f32[:, 0:NH], onef[0:1, :], lscT[0:1, :], start=True,
                 stop=True).then_inc(s["msc"].h, 1)
    s["msc"].inc()

    @P.on("vector")
    def _(e, w=s["msc"].n):
        e.wait_ge(s["msc"].h, w)
        e.tensor_copy(scbc[:, :], ps_f32[:, 0:NH])
        e.memset(bass.AP(Vx, 64, [[NT * NH * 65, 128], [65, NT * NH], [1, 1]]), 1.0)
        e.memset(nrm[0][:, :], 0.0)
        e.memset(nrm[1][:, :], 0.0)
        e.memset(epsT[:, :], EPS).then_inc(s["msc"].h, 1)
    s["msc"].inc()
    MSC_ALL = s["msc"].n

    for c in range(DC):
        stg = wstg[0]

        sm = "wkld" if c % 2 == 0 else "wkldB"

        @P.on("sync")
        def _(e, c=c, stg=stg, sm=sm, wfree=s["wcast"].n):
            if wfree > 0:
                e.wait_ge(s["wcast"].h, wfree)
            e.dma_start(out=stg[:, 0:256], in_=d_wq[c * 128:(c + 1) * 128, :]).then_inc(s[sm].h, 16)
            e.dma_start(out=stg[:, 256:512], in_=d_wk[c * 128:(c + 1) * 128, :]).then_inc(s[sm].h, 16)
            e.dma_start(out=stg[:, 512:768], in_=d_wv[c * 128:(c + 1) * 128, :]).then_inc(s[sm].h, 16)
        s[sm].inc(48)

        @P.on("vector")
        def _(e, c=c, stg=stg, sm=sm, w=s[sm].n):
            e.wait_ge(s[sm].h, w)
            e.tensor_copy(wqkv[:, c * 768:(c + 1) * 768], stg[:, :]).then_inc(s["wcast"].h, 1)
        s["wcast"].inc()

    def trig_nsub(b):
        ntl = min(TB, NT - b * TB)
        return (ntl * 128 + 255) // 256

    def emit_trig_sub(b, sub):
        ntl = min(TB, NT - b * TB)
        smt = "th" if b % 2 == 0 else "thB"
        if sub == 0:
            @P.on("sync")
            def _(e, b=b, ntl=ntl, smt=smt, wtr=s["trig"].n):
                if b >= 2:
                    e.wait_ge(s["trig"].h, wtr)
                e.dma_start(out=th0b[b % 2][:, 0:ntl * 32],
                            in_=d_th0[:, b * TB * 32:(b * TB + ntl) * 32]).then_inc(s[smt].h, 16)
            s[smt].inc(16)
        ncols = min(256, ntl * 128 - sub * 256)

        @P.on("vector")
        def _(e, b=b, sub=sub, ncols=ncols, smt=smt, wth=s[smt].n, wm=s["msc"].n,
              wtr=s["trig"].n - 1):
            e.wait_ge(s[smt].h, wth)
            e.wait_ge(s["msc"].h, wm)
            if wtr > 0:
                e.wait_ge(s["trig"].h, wtr)
            if b >= 2:
                e.wait_ge(s["rope"].h, (b - 1) * TB)
            ntile4 = ncols // 128
            for h in range(NH):
                ov = bass.AP(thv, h * 32, [[256, 128], [128, ntile4], [1, 32]])
                iv = bass.AP(th0b[b % 2], sub * 64, [[TB * 32, 128], [32, ntile4], [1, 32]])
                e.tensor_scalar(ov, iv, scbc[:, h:h + 1], None, ALU.mult)
            c = slice(0, ncols)
            e.tensor_scalar(tu1[:, c], thv[:, c], INV2PI, None, ALU.mult)
            e.tensor_copy(tki[:, c], tu1[:, c])
            e.tensor_copy(tkf[:, c], tki[:, c])
            e.scalar_tensor_tensor(tu1[:, c], tkf[:, c], -C1, thv[:, c], ALU.mult, ALU.add)
            e.scalar_tensor_tensor(tu2[:, c], tkf[:, c], -C2, tu1[:, c], ALU.mult, ALU.add)
            e.scalar_tensor_tensor(tg1[:, c], tkf[:, c], -C3, tu2[:, c], ALU.mult, ALU.add)
            e.tensor_scalar(tg2[:, c], tg1[:, c], PI, None, ALU.is_gt)
            e.tensor_scalar(tg3[:, c], tg1[:, c], -PI, None, ALU.is_lt)
            e.tensor_tensor(tu1[:, c], tg3[:, c], tg2[:, c], ALU.subtract)
            e.scalar_tensor_tensor(tu2[:, c], tu1[:, c], C1, tg1[:, c], ALU.mult, ALU.add)
            e.scalar_tensor_tensor(tg3[:, c], tu1[:, c], C23, tu2[:, c], ALU.mult, ALU.add)
            e.tensor_scalar(trs[sub % 2][:, c], tg3[:, c], -PI_IN, PI_IN, ALU.max, ALU.min)
            e.tensor_scalar(tu1[:, c], trs[sub % 2][:, c], PI / 2, None, ALU.add)
            e.tensor_scalar(tg2[:, c], tu1[:, c], PI, None, ALU.is_gt)
            e.scalar_tensor_tensor(tu2[:, c], tg2[:, c], -C1, tu1[:, c], ALU.mult, ALU.add)
            e.scalar_tensor_tensor(tg3[:, c], tg2[:, c], -C23, tu2[:, c], ALU.mult, ALU.add)
            e.tensor_scalar(trc[sub % 2][:, c], tg3[:, c], -PI_IN, PI_IN,
                            ALU.max, ALU.min).then_inc(s["sin"].h, 1)
        s["sin"].inc()

        @P.on("scalar")
        def _(e, b=b, sub=sub, ncols=ncols, w=s["sin"].n):
            e.wait_ge(s["sin"].h, w)
            e.activation(sinr[b % 2][:, sub * 256:sub * 256 + ncols],
                         trs[sub % 2][:, 0:ncols], AF.Sin)
            e.activation(cosr[b % 2][:, sub * 256:sub * 256 + ncols],
                         trc[sub % 2][:, 0:ncols], AF.Sin).then_inc(s["trig"].h, 1)
        s["trig"].inc()

    pref_qev, pref_kev, pref_vev, pref_nfz = {}, {}, {}, {}
    pref_sc2, pref_exp = {}, {}
    chst = dict(pend_tail=[], pend_fin=None, sc_free=[0, 0], pt_free=[0, 0, 0])
    PVLAG = 2

    def tile_gen(i):
        j = i % 2
        if i == 0:
            for _sub in range(trig_nsub(0)):
                emit_trig_sub(0, _sub)
        bnext = i // TB + 1
        if bnext < NBATCH and i % TB >= TB // 2:
            _sub = i % TB - TB // 2
            if _sub < trig_nsub(bnext):
                emit_trig_sub(bnext, _sub)

        smx = "x" if j == 0 else "xB"

        @P.on("gpsimd")
        def _(e, i=i, j=j, smx=smx):
            if i >= 2:
                e.wait_ge(s["ln"].h, i - 1)
            e.dma_start(out=xstg[j][:, :], in_=d_x[i * 128:(i + 1) * 128, :]).then_inc(s[smx].h, 16)
        s[smx].inc(16)

        sq_waits = [(smx, s[smx].n)] + ([("sqx", s["sqx"].n)] if i >= 1 else [])
        red_waits = [("tpx", s["tpx"].n - 1)] if i >= 2 else []
        emit_ln(P, s, "sqx", xstg[j][:, :], xsq[:, :], lsm, epsT, sq_waits, red_waits,
                chain_waits=([("ln", i)] if i >= 1 else []))

        @P.on("vector")
        def _(e, j=j, w=s["sqx"].n):
            e.wait_ge(s["sqx"].h, w)
            e.tensor_scalar(xhat[j][:, :], xstg[j][:, :], lsm[:, 9:10], lsm[:, 8:9],
                            ALU.add, ALU.mult).then_inc(s["ln"].h, 1)
        s["ln"].inc()

        yield

        @P.on("tensor")
        def _(e, i=i, j=j, wl=s["ln"].n, wx=s["xhT"].n, wk=s["kev"].n, wq=s["qev"].n):
            e.wait_ge(s["ln"].h, wl)
            if i >= 1:
                e.wait_ge(s["xhT"].h, wx)
                if wk > 0:
                    e.wait_ge(s["kev"].h, wk)
                if wq > 0:
                    e.wait_ge(s["qev"].h, wq)
            last = None
            for c in range(DC):
                last = e.transpose(ps_bf[:, c * 128:(c + 1) * 128],
                                   xhat[j][:, c * 128:(c + 1) * 128], idb[:, :])
            last.then_inc(s["tpx"].h, 1)
        s["tpx"].inc()

        @P.on("scalar")
        def _(e, j=j, w=s["tpx"].n):
            e.wait_ge(s["tpx"].h, w)
            e.copy(xhT[j][:, :], ps_bf[:, :]).then_inc(s["xhT"].h, 1)
        s["xhT"].inc()

        @P.on("tensor")
        def _(e, i=i, j=j, wx=s["xhT"].n, wr=s["rope"].n, wv=s["vev"].n,
              wrb=s["rb"].n, wnf=s["nfz"].n, wc=s["wcast"].n, wm=MSC_ALL):
            e.wait_ge(s["xhT"].h, wx)
            if wr > 0:
                e.wait_ge(s["rope"].h, wr)
            if wv > 0:
                e.wait_ge(s["vev"].h, wv)
            if wrb > 0:
                e.wait_ge(s["rb"].h, wrb)
            if wnf > 0:
                e.wait_ge(s["nfz"].h, wnf)
            if i == 0:
                e.wait_ge(s["msc"].h, wm)
            if i >= 2:
                e.wait_ge(s["wcast"].h, wc)
            last = None
            for c in range(DC):
                if i < 2:
                    e.wait_ge(s["wcast"].h, c + 1)
                st, sp = (c == 0), (c == DC - 1)
                last = e.matmul(ps_f32[:, 0:512], xhT[j][:, c * 128:(c + 1) * 128],
                                wqkv[:, c * 768:c * 768 + 512], start=st, stop=sp)
            last.then_inc(s["qkv"].h, 1)
        s["qkv"].inc()

        yield

        @P.on("vector")
        def _(e, i=i, j=j, wq_=s["qkv"].n, ws=s["trig"].n, wk=s["ktr"].n,
              wqt=s["qtr"].n, wsq=s["sq"].n):
            e.wait_ge(s["qkv"].h, wq_)
            e.wait_ge(s["trig"].h, ws)
            if i >= 2:
                e.wait_ge(s["ktr"].h, wk - 1)
                e.wait_ge(s["qtr"].h, wqt - 1)
                e.wait_ge(s["sq"].h, wsq - 1)
            rb, ro = (i // TB) % 2, (i % TB) * 128
            cosv = bass.AP(cosr[rb], ro, [[TB * 128, 128], [32, NH], [1, 32]])
            sinv = bass.AP(sinr[rb], ro, [[TB * 128, 128], [32, NH], [1, 32]])
            last = None
            for (base, dst) in ((0, Qtm[j]), (256, krot[j])):
                x1 = bass.AP(ps_f32, base, [[512, 128], [64, NH], [2, 32]])
                x2 = bass.AP(ps_f32, base + 1, [[512, 128], [64, NH], [2, 32]])
                o1 = bass.AP(dst, 0, [[256, 128], [64, NH], [1, 32]])
                o2 = bass.AP(dst, 32, [[256, 128], [64, NH], [1, 32]])
                va = bass.AP(tmpa, 0, [[128, 128], [32, NH], [1, 32]])
                vb = bass.AP(tmpb, 0, [[128, 128], [32, NH], [1, 32]])
                e.tensor_tensor(va, x1, cosv, ALU.mult)
                e.tensor_tensor(vb, x2, sinv, ALU.mult)
                e.tensor_tensor(o1, va, vb, ALU.subtract)
                e.tensor_tensor(va, x1, sinv, ALU.mult)
                e.tensor_tensor(vb, x2, cosv, ALU.mult)
                last = e.tensor_tensor(o2, va, vb, ALU.add)
            last.then_inc(s["rope"].h, 1)
        s["rope"].inc()

        @P.on("tensor")
        def _(e, j=j, wr=s["rope"].n, wrb=s["rb"].n, wnf=s["nfz"].n):
            e.wait_ge(s["rope"].h, wr)
            if wrb > 0:
                e.wait_ge(s["rb"].h, wrb)
            if wnf > 0:
                e.wait_ge(s["nfz"].h, wnf)
            last = None
            for c in range(DC):
                st, sp = (c == 0), (c == DC - 1)
                last = e.matmul(ps_f32[:, 0:256], xhT[j][:, c * 128:(c + 1) * 128],
                                wqkv[:, c * 768 + 512:c * 768 + 768], start=st, stop=sp)
            last.then_inc(s["vmm"].h, 1)
        s["vmm"].inc()

        @P.on("vector")
        def _(e, i=i, w=s["vmm"].n):
            e.wait_ge(s["vmm"].h, w)
            vin = bass.AP(ps_f32, 0, [[512, 128], [64, NH], [1, 64]])
            vout = bass.AP(Vx, i * NH * 65, [[NT * NH * 65, 128], [65, NH], [1, 64]])
            e.tensor_copy(vout, vin).then_inc(s["vev"].h, 1)
        s["vev"].inc()

        @P.on("scalar")
        def _(e, i=i, j=j, w=s["rope"].n, wn=s["nrm"].n):
            e.wait_ge(s["rope"].h, w)
            if i >= 2:
                e.wait_ge(s["nrm"].h, wn - 1)
            e.activation(sqs[j][:, 0:256], Qtm[j][:, :], AF.Square)
            e.activation(sqs[j][:, 256:512], krot[j][:, :], AF.Square).then_inc(s["sq"].h, 1)
        s["sq"].inc()

        yield

        @P.on("vector")
        def _(e, i=i, j=j, w=s["sq"].n):
            e.wait_ge(s["sq"].h, w)
            v3 = bass.AP(sqs[j], 0, [[512, 128], [64, 8], [1, 64]])
            e.tensor_reduce(nrw[i % 2][:, :], v3, AX, ALU.add)
            if i >= 1:
                e.tensor_tensor(nrm[(i - 1) % 2][:, :], nrm[i % 2][:, :],
                                nrw[(i - 1) % 2][:, :], ALU.max)
            e.nop().then_inc(s["nrm"].h, 1)
        s["nrm"].inc()

        @P.on("tensor")
        def _(e, j=j, wr=s["rope"].n, wx=s["xhT"].n):
            e.wait_ge(s["rope"].h, wr)
            e.wait_ge(s["xhT"].h, wx)
            last = None
            for h in range(NH):
                last = e.transpose(ps_bf[0:64, h * 128:(h + 1) * 128],
                                   krot[j][:, h * 64:(h + 1) * 64], idb[:, :])
            last.then_inc(s["ktr"].h, 1)
        s["ktr"].inc()

        @P.on("vector")
        def _(e, i=i, w=s["ktr"].n):
            e.wait_ge(s["ktr"].h, w)
            kin = bass.AP(ps_bf, 0, [[1024, 64], [128, NH], [1, 128]])
            kout = bass.AP(Kfm, i * 128, [[NH * S, 64], [S, NH], [1, 128]])
            e.tensor_copy(kout, kin).then_inc(s["kev"].h, 1)
        s["kev"].inc()

        @P.on("tensor")
        def _(e, j=j, wr=s["rope"].n, wx=s["xhT"].n):
            e.wait_ge(s["rope"].h, wr)
            e.wait_ge(s["xhT"].h, wx)
            last = None
            for h in range(NH):
                last = e.transpose(ps_bf[0:64, 512 + h * 128:512 + (h + 1) * 128],
                                   Qtm[j][:, h * 64:(h + 1) * 64], idb[:, :])
            last.then_inc(s["qtr"].h, 1)
        s["qtr"].inc()

        @P.on("vector")
        def _(e, i=i, w=s["qtr"].n, wsc=pref_sc2.get(i // KPC - 2, 0)):
            e.wait_ge(s["qtr"].h, w)
            if wsc > 0:
                e.wait_ge(s["sc"].h, wsc)
            qin = bass.AP(ps_bf, 512, [[1024, 64], [128, NH], [1, 128]])
            qout = bass.AP(qall[(i // KPC) % 2], (i % KPC) * 128,
                           [[NH * QC, 64], [QC, NH], [1, 128]])
            e.tensor_copy(qout, qin).then_inc(s["qev"].h, 1)
        s["qev"].inc()

    def emit_finalize(q):
        t = (q + 1) * KPC - 1

        @P.on("vector")
        def _(e, t=t, w=s["nrm"].n):
            e.wait_ge(s["nrm"].h, w)
            e.tensor_tensor(nfin[:, :], nrm[(t - 1) % 2][:, :], nrw[t % 2][:, :],
                            ALU.max).then_inc(s["bar"].h, 1)
        s["bar"].inc()

        @P.on("tensor")
        def _(e, w=s["bar"].n, wv=s["vev"].n, wrb=s["rb"].n):
            e.wait_ge(s["bar"].h, w)
            e.wait_ge(s["vev"].h, wv)
            if wrb > 0:
                e.wait_ge(s["rb"].h, wrb)
            e.transpose(ps_f32[0:8, 0:128], nfin[:, 0:8], idf[:, :]).then_inc(s["bar"].h, 1)
        s["bar"].inc()

        @P.on("vector")
        def _(e, w=s["bar"].n):
            e.wait_ge(s["bar"].h, w)
            e.tensor_copy(nfs[0:8, 0:128], ps_f32[0:8, 0:128])
            e.tensor_copy(nfs[0:16, 256:384], idf[0:16, 0:128])
            e.reduce_max(nfs[0:8, 128:129], nfs[0:8, 0:128], axis=AX).then_inc(s["bar"].h, 1)
        s["bar"].inc()

        @P.on("scalar")
        def _(e, w=s["bar"].n):
            e.wait_ge(s["bar"].h, w)
            e.activation(nfs[0:8, 129:130], nfs[0:8, 128:129], AF.Sqrt).then_inc(s["bar"].h, 1)
        s["bar"].inc()

        @P.on("tensor")
        def _(e, w=s["bar"].n):
            e.wait_ge(s["bar"].h, w)
            e.transpose(ps_f32[0:1, 0:8], nfs[0:8, 129:130], idf[0:8, 0:8]).then_inc(s["bar"].h, 1)
        s["bar"].inc()

        @P.on("vector")
        def _(e, w=s["bar"].n):
            e.wait_ge(s["bar"].h, w)
            e.tensor_copy(nfs[0:1, 132:140], ps_f32[0:1, 0:8])
            e.tensor_copy(nfs[0:16, 256:384], idf[0:16, 0:128])
            e.tensor_copy(nfs[0:16, 384:512], idf[0:16, 0:128])
            e.tensor_tensor(nfs[0:1, 140:140 + NH], nfs[0:1, 132:132 + NH],
                            nfs[0:1, 136:136 + NH], ALU.mult)
            e.tensor_copy(nfs[0:16, 256:384], idf[0:16, 0:128])
            e.tensor_copy(nfs[0:16, 384:512], idf[0:16, 0:128])
            e.tensor_scalar(nfs[0:1, 148:148 + NH], nfs[0:1, 140:140 + NH],
                            -0.125, None, ALU.mult)
            e.tensor_copy(nfs[0:16, 256:384], idf[0:16, 0:128])
            e.tensor_copy(nfs[0:16, 384:512], idf[0:16, 0:128]).then_inc(s["bar"].h, 1)
        s["bar"].inc()

        @P.on("tensor")
        def _(e, w=s["bar"].n):
            e.wait_ge(s["bar"].h, w)
            e.matmul(ps_f32[:, 0:NH], onef[0:1, :], nfs[0:1, 148:148 + NH],
                     start=True, stop=True).then_inc(s["bar"].h, 1)
        s["bar"].inc()

        @P.on("vector")
        def _(e, q=q, w=s["bar"].n, we=pref_exp.get(q - 2, 0)):
            e.wait_ge(s["bar"].h, w)
            if we > 0:
                e.wait_ge(s["exp"].h, we)
            e.tensor_copy(negM[:, (q % 2) * NH:(q % 2) * NH + NH],
                          ps_f32[:, 0:NH]).then_inc(s["nfz"].h, 1)
        s["nfz"].inc()

        pref_qev[q] = s["qev"].n
        pref_kev[q] = s["kev"].n
        pref_vev[q] = s["vev"].n
        pref_nfz[q] = s["nfz"].n

    def emit_tail_piece(args):
        kind, h, qc, cpar, o = args
        if kind == "bc":
            @P.on("tensor")
            def _(e, o=o, w=s["rcp"].n, wrb=s["rb"].n, wv=s["vmm"].n,
                  wr=s["qkv"].n, wnf=s["nfz"].n):
                e.wait_ge(s["rcp"].h, w)
                if wrb > 0:
                    e.wait_ge(s["rb"].h, wrb)
                e.wait_ge(s["vev"].h, wv)
                e.wait_ge(s["rope"].h, wr)
                if wnf > 0:
                    e.wait_ge(s["nfz"].h, wnf)
                e.matmul(ps_f32[0:64, 0:512], onef[0:1, 0:64],
                         rcpr[0:1, QC + o:QC + o + 512],
                         start=True, stop=True).then_inc(s["bc"].h, 1)
            s["bc"].inc()

            @P.on("vector")
            def _(e, cpar=cpar, o=o, w=s["bc"].n, wdma=s["aodma"].n):
                e.wait_ge(s["bc"].h, w)
                if wdma > 0:
                    e.wait_ge(s["aodma"].h, wdma)
                e.tensor_copy(rbt[:, :], ps_f32[0:64, 0:512]).then_inc(s["rb"].h, 1)
                e.tensor_tensor(aos[cpar][:, o:o + 512], ps_pv[0:64, o:o + 512],
                                rbt[:, :], ALU.mult).then_inc(s["ao"].h, 1)
            s["rb"].inc()
            s["ao"].inc()
        else:
            @P.on("gpsimd")
            def _(e, h=h, qc=qc, cpar=cpar, w=s["ao"].n):
                e.wait_ge(s["ao"].h, w)
                quarter = qc // (NQC // 4)
                coff = (qc % (NQC // 4)) * QC
                for jj in (quarter, 4 + quarter):
                    e.dma_start(
                        out=bass.AP(t_a2i, (jj * NH * 64 + h * 64) * TQ + coff,
                                    [[TQ, 64], [1, QC]]),
                        in_=aos[cpar][:, :]).then_inc(s["aodma"].h, 16)
            s["aodma"].inc(32)

    def emit_pend_fin():
        pf = chst["pend_fin"]
        if pf is None:
            return
        chst["pend_fin"] = None
        h, qc, cpar = pf

        @P.on("vector")
        def _(e, w=s["pv"].n, wr=s["rcp"].n):
            e.wait_ge(s["pv"].h, w)
            if wr > 0:
                e.wait_ge(s["rcp"].h, wr)
            e.tensor_copy(dnm[0:1, :], ps_pv[64:65, :]).then_inc(s["dnm"].h, 1)
        s["dnm"].inc()

        @P.on("scalar")
        def _(e, w=s["dnm"].n, wbc=s["bc"].n):
            e.wait_ge(s["dnm"].h, w)
            if wbc > 0:
                e.wait_ge(s["bc"].h, wbc)
            e.activation(rcpr[0:1, 0:QC], dnm[0:1, :], AF.Ln)
            e.activation(rcpr[0:1, QC:2 * QC], rcpr[0:1, 0:QC], AF.Exp,
                         scale=-1.0).then_inc(s["rcp"].h, 1)
        s["rcp"].inc()

        chst["pend_tail"] = [("bc", h, qc, cpar, half * 512) for half in range(NB)]
        chst["pend_tail"].append(("dma", h, qc, cpar, 0))

    def chunk_gen(c):
        emit_pend_fin()
        qc, h = c // NH, c % NH
        cpar = c % 2
        nkb = (qc + 1) * KPC
        dstart = qc * KPC
        order = list(range(dstart, nkb)) + list(range(0, dstart))
        first_b, last_b = {}, {}
        for ki, kb in enumerate(order):
            diag = kb >= dstart
            b0 = ((kb - dstart) * 128) // 512 if diag else 0
            for b in range(b0, NB):
                if b not in first_b:
                    first_b[b] = ki
                last_b[b] = ki
        pend_pv = []

        def emit_pv(spec):
            ki, kb, off, nblk, dep = spec
            kpar4 = ki % 3

            @P.on("tensor")
            def _(e, dep=dep, ki=ki, kb=kb, off=off, nblk=nblk, kpar4=kpar4,
                  wao=s["ao"].n if ki == 0 else 0,
                  wvv=pref_vev[qc] if ki == 0 else 0, wm=MSC_ALL):
                e.wait_ge(s[dep[0]].h, dep[1])
                if ki == 0:
                    if wao > 0:
                        e.wait_ge(s["ao"].h, wao)
                    e.wait_ge(s["vev"].h, wvv)
                    e.wait_ge(s["msc"].h, wm)
                last = None
                for b in range(nblk):
                    o = off + b * 512
                    bi = o // 512
                    last = e.matmul(ps_pv[:, o:o + 512],
                                    Vx[:, kb * NH * 65 + h * 65: kb * NH * 65 + h * 65 + 65],
                                    pt[kpar4][:, o:o + 512],
                                    start=(first_b[bi] == ki), stop=(last_b[bi] == ki))
                last.then_inc(s["pv"].h, 1)
            s["pv"].inc()
            chst["pt_free"][kpar4] = s["pv"].n

        for ki, kb in enumerate(order):
            kpar = ki % 2
            kpar4 = ki % 3
            diag = kb >= dstart
            c0 = (kb - dstart) * 128 if diag else 0
            b0 = (c0 // 512) if diag else 0
            off = 512 * b0
            nblk = NB - b0

            @P.on("tensor")
            def _(e, h=h, qc=qc, kb=kb, ki=ki, kpar=kpar, off=off, nblk=nblk,
                  wexp=chst["sc_free"][kpar],
                  wqv=pref_qev[qc] if ki == 0 else 0,
                  wkv=pref_kev[qc] if ki == 0 else 0):
                if ki == 0:
                    e.wait_ge(s["qev"].h, wqv)
                    e.wait_ge(s["kev"].h, wkv)
                if wexp > 0:
                    e.wait_ge(s["exp"].h, wexp)
                last = None
                for b in range(nblk):
                    o = off + b * 512
                    last = e.matmul(ps_sc[kpar][:, o:o + 512],
                                    Kfm[:, h * S + kb * 128: h * S + (kb + 1) * 128],
                                    qall[qc % 2][:, h * QC + o: h * QC + o + 512],
                                    start=True, stop=True)
                last.then_inc(s["sc"].h, 1)
            s["sc"].inc()

            if chst["pend_tail"] and 1 <= ki <= len(chst["pend_tail"]):
                emit_tail_piece(chst["pend_tail"][ki - 1])
                if ki == len(chst["pend_tail"]):
                    chst["pend_tail"] = []

            @P.on("scalar")
            def _(e, h=h, qc=qc, ki=ki, kpar=kpar, kpar4=kpar4, off=off,
                  w=s["sc"].n, wpv=chst["pt_free"][kpar4],
                  wnf=pref_nfz[qc] if ki == 0 else 0):
                if ki == 0:
                    e.wait_ge(s["nfz"].h, wnf)
                e.wait_ge(s["sc"].h, w)
                if wpv > 0:
                    e.wait_ge(s["pv"].h, wpv)
                e.activation(pt[kpar4][:, off:QC], ps_sc[kpar][:, off:QC], AF.Exp,
                             bias=negM[:, (qc % 2) * NH + h:(qc % 2) * NH + h + 1],
                             scale=0.125).then_inc(s["exp"].h, 1)
            s["exp"].inc()
            chst["sc_free"][kpar] = s["exp"].n

            dep = ("exp", s["exp"].n)
            if diag:
                L = c0 + 128 - off

                @P.on("vector")
                def _(e, kpar4=kpar4, off=off, L=L, w=s["exp"].n):
                    e.wait_ge(s["exp"].h, w)
                    e.tensor_tensor(ptm[:, 0:L], pt[kpar4][:, off:off + L],
                                    mkz[:, 512 - L:512], ALU.mult)
                    e.tensor_copy(pt[kpar4][:, off:off + L], ptm[:, 0:L]).then_inc(s["msk"].h, 1)
                s["msk"].inc()
                dep = ("msk", s["msk"].n)

            pend_pv.append((ki, kb, off, nblk, dep))
            if len(pend_pv) > PVLAG:
                emit_pv(pend_pv.pop(0))
            yield

        for piece in chst["pend_tail"]:
            emit_tail_piece(piece)
        chst["pend_tail"] = []
        for spec in pend_pv:
            emit_pv(spec)

        chst["pend_fin"] = (h, qc, cpar)
        pref_exp[qc] = s["exp"].n
        if h == NH - 1:
            pref_sc2[qc] = s["sc"].n

    ready = []
    sched = dict(active=None, done=0)
    GAPCAP = 4 * KPC
    GAP_IN0 = 0
    GAP_IN1 = 0

    def emit_quantum(cap):
        n = 0
        while n < cap and (sched["active"] is not None or ready):
            if sched["active"] is None:
                sched["active"] = chunk_gen(ready.pop(0))
            try:
                next(sched["active"])
                n += 1
            except StopIteration:
                sched["active"] = None
                sched["done"] += 1

    def force_chunks(n_done):
        while sched["done"] < n_done and (sched["active"] is not None or ready):
            emit_quantum(1)

    tgs = [tile_gen(i) for i in range(NT)]
    next(tgs[0])
    for i in range(NT):
        if i % KPC == 0:
            force_chunks((i // KPC - 1) * NH)
        next(tgs[i])
        if i + 1 < NT:
            next(tgs[i + 1])
        emit_quantum(GAP_IN0)
        next(tgs[i])
        emit_quantum(GAP_IN1)
        for _ in tgs[i]:
            pass
        if (i + 1) % KPC == 0:
            q = (i + 1) // KPC - 1
            force_chunks((q - 1) * NH)
            emit_finalize(q)
            for h in range(NH):
                ready.append(q * NH + h)
        emit_quantum(GAPCAP)
    emit_quantum(10 ** 9)
    emit_pend_fin()
    for piece in chst["pend_tail"]:
        emit_tail_piece(piece)
    chst["pend_tail"] = []

    if dbg:
        @P.on("gpsimd")
        def _(e, w=s["nfz"].n, w2=s["kev"].n, w3=s["vev"].n):
            e.wait_ge(s["nfz"].h, w)
            e.wait_ge(s["kev"].h, w2)
            e.wait_ge(s["vev"].h, w3)
            e.dma_start(out=d_okfm[:, :], in_=Kfm[:, :]).then_inc(s["outd"].h, 16)
            e.dma_start(out=d_ovx[:, :], in_=Vx[:, :]).then_inc(s["outd"].h, 16)
            e.dma_start(out=d_onegm[:, :], in_=negM[:, :]).then_inc(s["outd"].h, 16)
        s["outd"].inc(48)

    p3_done = dict(aodma=s["aodma"].n)
    P.flush()
    es0.close()
    esM.close()

    es4 = ExitStack()
    E4 = es4.enter_context
    wfst = [E4(nc.sbuf_tensor(f"wfst{j}", [128, 1024], F32)) for j in range(2)]
    wo_bf = E4(nc.sbuf_tensor("wo_bf", [128, DC * 1024], BF16))
    wost = [E4(nc.sbuf_tensor(f"wost{j}", [128, 1024], F32)) for j in range(2)]
    dsh2 = E4(nc.sbuf_tensor("dsh2", [128, 1024], F32))
    weff = E4(nc.sbuf_tensor("weff", [128, DC * 1024], BF16))
    ao = E4(nc.sbuf_tensor("ao", [128, DC * TQ], BF16))
    x1 = E4(nc.sbuf_tensor("x1", [128, NTQ * D], F32))
    xrs = [E4(nc.sbuf_tensor(f"xrs{j}", [128, D], F32)) for j in range(2)]
    ysq2 = E4(nc.sbuf_tensor("ysq2", [128, D], F32))
    ysq3 = E4(nc.sbuf_tensor("ysq3", [128, D], F32))
    ysq4 = E4(nc.sbuf_tensor("ysq4", [128, D], F32))
    lsm2 = E4(nc.sbuf_tensor("lsm2", [128, 16], F32))
    lsm3 = E4(nc.sbuf_tensor("lsm3", [128, 16], F32))
    lsm4 = E4(nc.sbuf_tensor("lsm4", [128, 16], F32))
    xh2 = [E4(nc.sbuf_tensor(f"xh2{j}", [128, D], BF16)) for j in range(2)]
    xh2T = [E4(nc.sbuf_tensor(f"xh2T{j}", [128, D], BF16)) for j in range(2)]
    outt = [E4(nc.sbuf_tensor(f"outt{j}", [128, D], F32)) for j in range(2)]
    ps_y = [E4(nc.psum_tensor(f"ps_y{j}", [128, 1024], F32)) for j in range(2)]
    ps_t2 = [E4(nc.psum_tensor(f"ps_t2{j}", [128, D], BF16)) for j in range(2)]

    @P.on("gpsimd")
    def _(e, w=p3_done["aodma"]):
        e.wait_ge(s["aodma"].h, w)
        with allow_ici_all_to_all():
            e.collective_compute(
                "AllToAll", ALU.bypass,
                replica_groups=[[0, 1, 2, 3, 4, 5, 6, 7]],
                ins=[t_a2i.ap().opt()],
                outs=[t_a2o.ap().opt()],
            ).then_inc(s["cc"].h, 1)
    s["cc"].inc()

    @P.on("sync")
    def _(e, w=p3_done["aodma"]):
        e.wait_ge(s["aodma"].h, w)
        e.dma_start(out=dsh2[0:64, :], in_=d_dsh[:, :]).then_inc(s["dshld"].h, 16)
        e.dma_start(out=dsh2[64:128, :], in_=d_dsh[:, :]).then_inc(s["dshld"].h, 16)
    s["dshld"].inc(32)

    for jj in range(DC):
        @P.on("sync")
        def _(e, jj=jj, smf="wfld" if jj % 2 == 0 else "wfldB", wfree=s["wft"].n - 1):
            if wfree > 0:
                e.wait_ge(s["wft"].h, wfree)
            e.dma_start(out=wfst[jj % 2][:, :],
                        in_=d_wf[jj * 128:(jj + 1) * 128, :]).then_inc(s[smf].h, 16)
        smf = "wfld" if jj % 2 == 0 else "wfldB"
        s[smf].inc(16)

        @P.on("tensor")
        def _(e, jj=jj, smf=smf, w=s[smf].n, wv=s["weff"].n - 1):
            e.wait_ge(s[smf].h, w)
            if jj >= 2:
                e.wait_ge(s["weff"].h, wv)
            last = None
            for c in range(DC):
                last = e.transpose(ps_y[jj % 2][:, c * 128:(c + 1) * 128],
                                   wfst[jj % 2][:, c * 128:(c + 1) * 128],
                                   idf[:, :])
            last.then_inc(s["wft"].h, 1)
        s["wft"].inc()

        @P.on("vector")
        def _(e, jj=jj, w=s["wft"].n, wd=s["dshld"].n):
            e.wait_ge(s["wft"].h, w)
            e.wait_ge(s["dshld"].h, wd)
            for c in range(DC):
                ov = weff[:, c * 1024 + jj * 128: c * 1024 + (jj + 1) * 128]
                e.tensor_tensor(ov, ps_y[jj % 2][:, c * 128:(c + 1) * 128],
                                dsh2[:, jj * 128:(jj + 1) * 128], ALU.add)
            e.nop().then_inc(s["weff"].h, 1)
        s["weff"].inc()

    for c in range(DC):
        stg = wost[c % 2]

        smo = "wo_ld" if c % 2 == 0 else "wo_ldB"

        @P.on("sync")
        def _(e, c=c, stg=stg, smo=smo, wfree=s["wo_c"].n - 1):
            if wfree > 0:
                e.wait_ge(s["wo_c"].h, wfree)
            e.dma_start(out=stg[:, :], in_=d_wo[c * 128:(c + 1) * 128, :]).then_inc(s[smo].h, 16)
        s[smo].inc(16)

        @P.on("vector")
        def _(e, c=c, stg=stg, smo=smo, w=s[smo].n):
            e.wait_ge(s[smo].h, w)
            e.tensor_copy(wo_bf[:, c * 1024:(c + 1) * 1024], stg[:, :]).then_inc(s["wo_c"].h, 1)
        s["wo_c"].inc()

    @P.on("gpsimd")
    def _(e, w=s["cc"].n, wv=s["weff"].n, wl=16 * 6):
        e.wait_ge(s["wld"].h, wl)
        e.wait_ge(s["cc"].h, w)
        e.wait_ge(s["weff"].h, wv)
        e.wait_ge(s["wo_c"].h, DC)
        with e.register("gb0") as r0, e.register("gb1") as r1:
            e.reg_load(r0, gbt[0:1, 0:1])
            for c in range(DC):
                e.reg_mov(r1, c * 128 * TQ)
                e.reg_add(r1, r1, r0)
                e.dma_start(out=ao[:, c * TQ:(c + 1) * TQ],
                            in_=bass.AP(t_a2o, r1, [[TQ, 128], [1, TQ]])).then_inc(s["aold"].h, 16)
    s["aold"].inc(16 * DC)

    for i in range(NTQ):
        j = i % 2

        smr = "xr" if j == 0 else "xrB"

        @P.on("sync")
        def _(e, i=i, j=j, smr=smr, wfree=s["x1"].n - 1):
            if i >= 2:
                e.wait_ge(s["x1"].h, wfree)
            e.dma_start(out=xrs[j][:, :], in_=d_xres[i * 128:(i + 1) * 128, :]).then_inc(s[smr].h, 16)
        s[smr].inc(16)

        @P.on("tensor")
        def _(e, i=i, j=j, w=s["aold"].n, wfree=s["x1"].n - 1):
            e.wait_ge(s["aold"].h, w)
            if i >= 2:
                e.wait_ge(s["x1"].h, wfree)
            last = None
            for c in range(DC):
                for bl in range(2):
                    last = e.matmul(ps_y[j][:, bl * 512:(bl + 1) * 512],
                                    ao[:, c * TQ + i * 128: c * TQ + (i + 1) * 128],
                                    wo_bf[:, c * 1024 + bl * 512: c * 1024 + (bl + 1) * 512],
                                    start=(c == 0), stop=(c == DC - 1))
            last.then_inc(s["ymm"].h, 1)
        s["ymm"].inc()

        sqw = [("ymm", s["ymm"].n)] + ([("x1", s["x1"].n)] if i >= 1 else [])
        emit_ln(P, s, "sqx", ps_y[j][:, :], ysq2[:, :], lsm2, epsT, sqw, [],
                chain_waits=([("x1", s["x1"].n)] if i >= 1 else []))

        @P.on("vector")
        def _(e, i=i, j=j, smr=smr, w=s["sqx"].n, wx=s[smr].n):
            e.wait_ge(s["sqx"].h, w)
            e.wait_ge(s[smr].h, wx)
            e.tensor_scalar(ysq2[:, :], ps_y[j][:, :], lsm2[:, 9:10], lsm2[:, 8:9],
                            ALU.add, ALU.mult)
            e.tensor_tensor(x1[:, i * D:(i + 1) * D], xrs[j][:, :], ysq2[:, :],
                            ALU.add).then_inc(s["x1"].h, 1)
        s["x1"].inc()

    p4_done = s["x1"].n
    for i in range(NTQ):
        j = i % 2

        sqw = [("x1", i + 1)] + ([("xh2", s["xh2"].n - 2)] if i >= 1 else [])
        emit_ln(P, s, "xh2", x1[:, i * D:(i + 1) * D], ysq3[:, :], lsm3, epsT, sqw, [])

        @P.on("vector")
        def _(e, i=i, j=j, w=s["xh2"].n, wt=s["tp2"].n - 1):
            e.wait_ge(s["xh2"].h, w)
            if i >= 2:
                e.wait_ge(s["tp2"].h, wt)
            e.tensor_scalar(xh2[j][:, :], x1[:, i * D:(i + 1) * D], lsm3[:, 9:10],
                            lsm3[:, 8:9], ALU.add, ALU.mult).then_inc(s["xh2"].h, 1)
        s["xh2"].inc()

        @P.on("tensor")
        def _(e, i=i, j=j, w=s["xh2"].n, wfree=s["xh2T"].n - 1):
            e.wait_ge(s["xh2"].h, w)
            if i >= 2:
                e.wait_ge(s["xh2T"].h, wfree)
            last = None
            for c in range(DC):
                last = e.transpose(ps_t2[j][:, c * 128:(c + 1) * 128],
                                   xh2[j][:, c * 128:(c + 1) * 128], idb[:, :])
            last.then_inc(s["tp2"].h, 1)
        s["tp2"].inc()

        @P.on("scalar")
        def _(e, j=j, w=s["tp2"].n):
            e.wait_ge(s["tp2"].h, w)
            e.copy(xh2T[j][:, :], ps_t2[j][:, :]).then_inc(s["xh2T"].h, 1)
        s["xh2T"].inc()

        @P.on("tensor")
        def _(e, i=i, j=j, w=s["xh2T"].n, wfree=s["outp"].n - 1):
            e.wait_ge(s["xh2T"].h, w)
            if i < 2:
                e.wait_ge(s["x1"].h, p4_done)
            if i >= 2:
                e.wait_ge(s["outp"].h, wfree)
            last = None
            for c in range(DC):
                for bl in range(2):
                    last = e.matmul(ps_y[j][:, bl * 512:(bl + 1) * 512],
                                    xh2T[j][:, c * 128:(c + 1) * 128],
                                    weff[:, c * 1024 + bl * 512: c * 1024 + (bl + 1) * 512],
                                    start=(c == 0), stop=(c == DC - 1))
            last.then_inc(s["y2"].h, 1)
        s["y2"].inc()

        sqw = [("y2", s["y2"].n)] + ([("outp", s["outp"].n)] if i >= 1 else [])
        emit_ln(P, s, "y2", ps_y[j][:, :], ysq4[:, :], lsm4, epsT, sqw, [],
                chain_waits=([("outp", s["outp"].n)] if i >= 1 else []))

        smd = "outd" if j == 0 else "outdB"

        @P.on("vector")
        def _(e, i=i, j=j, smd=smd, w=s["y2"].n, wd=s["outd" if j == 0 else "outdB"].n):
            e.wait_ge(s["y2"].h, w)
            if i >= 2:
                e.wait_ge(s[smd].h, wd)
            e.tensor_scalar(ysq4[:, :], ps_y[j][:, :], lsm4[:, 9:10], lsm4[:, 8:9],
                            ALU.add, ALU.mult)
            e.tensor_tensor(outt[j][:, :], x1[:, i * D:(i + 1) * D], ysq4[:, :],
                            ALU.add).then_inc(s["outp"].h, 1)
        s["outp"].inc()

        @P.on("gpsimd")
        def _(e, i=i, j=j, smd=smd, w=s["outp"].n):
            e.wait_ge(s["outp"].h, w)
            e.dma_start(out=d_out[i * 128:(i + 1) * 128, :], in_=outt[j][:, :]).then_inc(s[smd].h, 16)
        s[smd].inc(16)

    if dbg:
        @P.on("gpsimd")
        def _(e, wa=s["aold"].n, wx=s["x1"].n, ww=s["weff"].n):
            e.wait_ge(s["aold"].h, wa)
            e.wait_ge(s["x1"].h, wx)
            e.wait_ge(s["weff"].h, ww)
            e.dma_start(out=d_oao[:, :], in_=ao[:, :]).then_inc(s["outd"].h, 16)
            e.dma_start(out=d_owob[:, :], in_=wo_bf[:, :]).then_inc(s["outd"].h, 16)
            e.dma_start(out=d_ox1[:, :], in_=x1[:, :]).then_inc(s["outd"].h, 16)
            e.dma_start(out=d_oweff[:, :], in_=weff[:, :]).then_inc(s["outd"].h, 16)
        s["outd"].inc(64)

    @P.on("gpsimd")
    def _(e, w=s["outd"].n, wb=s["outdB"].n):
        e.wait_ge(s["outd"].h, w)
        if wb > 0:
            e.wait_ge(s["outdB"].h, wb)

    P.flush()
    block_cm.__exit__(None, None, None)
    es4.close()
    es_all.close()
    return nc


def _make_aux(cfg):
    S, NT = cfg["S"], cfg["NT"]
    g = np.asarray(RIEMANN, dtype=np.float32)[:32]
    inv = (np.float32(1.0) / (g / g[0])).astype(np.float32)
    pos = np.arange(S, dtype=np.float32)
    th = pos[:, None] * inv[None, :]
    th0 = np.ascontiguousarray(th.reshape(NT, 128, 32).transpose(1, 0, 2).reshape(128, NT * 32))
    idb = np.eye(128, dtype=ml_dtypes.bfloat16)
    idf = np.eye(128, dtype=np.float32)
    mkz = np.zeros((128, 512), dtype=np.float32)
    mkz[:, 384:512] = (np.arange(128)[None, :] >= np.arange(128)[:, None])
    mkz = mkz.astype(ml_dtypes.bfloat16)
    one = np.ones((1, 128), dtype=np.float32)
    return th0, idb, idf, mkz, one


def kernel(L0_attn_x, L0_attn_w_qkv, L0_attn_w_o, L0_attn_log_scale,
           L0_ffn_weight, L0_ffn_disha, _cfg=None, _ret_nc=False, _dbg=False):
    cfg = _cfg or make_cfg()
    S, TQ = cfg["S"], cfg["TQ"]
    th0, idb, idf, mkz, one = _make_aux(cfg)
    x = np.asarray(L0_attn_x, dtype=np.float32)
    wqkv_f = np.asarray(L0_attn_w_qkv, dtype=np.float32)
    wo = np.ascontiguousarray(np.asarray(L0_attn_w_o, dtype=np.float32))
    lsc = np.asarray(L0_attn_log_scale, dtype=np.float32)
    wf = np.ascontiguousarray(np.asarray(L0_ffn_weight, dtype=np.float32))
    dsh = np.ascontiguousarray(np.asarray(L0_ffn_disha, dtype=np.float32))

    in_maps = []
    for c in range(8):
        B, R = c // 4, c % 4
        H0 = 4 * R
        in_maps.append({
            "x": np.ascontiguousarray(x[B]),
            "xres": np.ascontiguousarray(x[B, R * TQ:(R + 1) * TQ]),
            "wq": np.ascontiguousarray(wqkv_f[:, H0 * 64:H0 * 64 + 256]),
            "wk": np.ascontiguousarray(wqkv_f[:, 1024 + H0 * 64:1024 + H0 * 64 + 256]),
            "wv": np.ascontiguousarray(wqkv_f[:, 2048 + H0 * 64:2048 + H0 * 64 + 256]),
            "wo": wo, "wf": wf, "dsh": dsh,
            "lsc": np.ascontiguousarray(lsc[H0:H0 + 4].reshape(1, 4)),
            "th0": th0, "idb": idb, "idf": idf, "mkz": mkz, "one": one,
            "gb": np.array([[B * 4 * 256 * TQ]], dtype=np.int32),
        })
    nc = build_program(cfg, dbg=_dbg)
    if _ret_nc:
        return nc, in_maps
    res = run_bass_kernel_spmd(nc, in_maps, core_ids=list(range(8)))
    out = np.empty((2, S, 1024), dtype=np.float32)
    for c in range(8):
        B, R = c // 4, c % 4
        out[B, R * TQ:(R + 1) * TQ] = res.results[c]["out"]
    if _dbg:
        return out, res.results
    return out
